# version 01
# baseline (speedup 1.0000x reference)
import numpy as np
import ml_dtypes
from contextlib import ExitStack

import concourse.bass as bass
import concourse.mybir as mybir
from concourse.bass_utils import run_bass_kernel_spmd
from concourse.alu_op_type import AluOpType as OP

F32 = mybir.dt.float32
BF16 = mybir.dt.bfloat16
AF = mybir.ActivationFunctionType
AXL = mybir.AxisListType
BF = ml_dtypes.bfloat16

S = 2048
D = 2048
H, KVH, HD = 16, 4, 128
NO = 24
C = 128
NCH = S // C
HFF = 5632
NHC = HFF // 128
EPS_LN = 1e-5
EPS_ATT = 1e-6
EPS_Q = 1e-8
NW = HFF * D
GSTRIP = 512
NGS = NW // (128 * GSTRIP)
NGTOT = 3 * NGS

ARENA_F32 = 52984

PHASES = ["p0", "p1", "p2a", "p2b", "gamma", "xn2t", "p3", "p4", "end"]


class Reg:
    def __init__(self, arena, off, cols, name):
        self.arena, self.off, self.cols, self.name = arena, off, cols, name

    @property
    def f32(self):
        return self.arena.ap[:, self.off:self.off + self.cols]

    @property
    def b16(self):
        return self.arena.ap[:, self.off:self.off + self.cols].bitcast(BF16)

    def free(self):
        self.arena.free(self)


class Arena:
    def __init__(self, ap):
        self.ap = ap
        self.avail = [(0, ARENA_F32)]

    def alloc(self, cols, name=""):
        cols = (cols + 7) & ~7
        for i, (o, c) in enumerate(self.avail):
            if c >= cols:
                self.avail[i] = (o + cols, c - cols)
                if self.avail[i][1] == 0:
                    self.avail.pop(i)
                return Reg(self, o, cols, name)
        raise RuntimeError(f"arena OOM for {name}: need {cols} f32 cols, free={self.avail}")

    def alloc_b16(self, cols_b16, name=""):
        return self.alloc((cols_b16 + 1) // 2, name)

    def free(self, reg):
        self.avail.append((reg.off, reg.cols))
        self.avail.sort()
        merged = []
        for o, c in self.avail:
            if merged and merged[-1][0] + merged[-1][1] == o:
                merged[-1] = (merged[-1][0], merged[-1][1] + c)
            else:
                merged.append((o, c))
        self.avail = merged


class Sem:
    def __init__(self, nc, ctx, name):
        self.h = ctx.enter_context(nc.semaphore(name))
        self.v = 0

    def inc(self, instr, n=1):
        instr.then_inc(self.h, n)
        self.v += n
        return self.v


def build_nc(dbg=(), stop_after=None, p0_chunks=NCH, gamma_on=True, p4_rounds=2, p4_parts=7):
    dbg = set(dbg)
    _sa = PHASES.index(stop_after) if stop_after else PHASES.index("end")

    def live(ph):
        return PHASES.index(ph) <= _sa

    nc = bass.Bass("TRN2", target_bir_lowering=False)

    x0 = nc.dram_tensor("x0", [S, D], F32, kind="ExternalInput").ap()
    wqkv = nc.dram_tensor("wqkv", [NO, 16, 128, 128], F32, kind="ExternalInput").ap()
    wo = nc.dram_tensor("wo", [D, D], F32, kind="ExternalInput").ap()
    w1b = nc.dram_tensor("w1b", [NHC, D, 128], F32, kind="ExternalInput").ap()
    w2b = nc.dram_tensor("w2b", [NHC, D, 128], F32, kind="ExternalInput").ap()
    w3b = nc.dram_tensor("w3b", [2, NHC, 128, 1024], F32, kind="ExternalInput").ap()
    cosf = nc.dram_tensor("cosf", [128, S], BF16, kind="ExternalInput").ap()
    sinx = nc.dram_tensor("sinx", [128, S], BF16, kind="ExternalInput").ap()
    mask4 = nc.dram_tensor("mask4", [128, 512], BF16, kind="ExternalInput").ap()
    ident = nc.dram_tensor("ident", [128, 128], BF16, kind="ExternalInput").ap()
    y = nc.dram_tensor("y", [S, D], F32, kind="ExternalOutput").ap()
    x1s = nc.dram_tensor("x1s", [S, D], F32, kind="Internal").ap()
    hids = nc.dram_tensor("hids", [NHC, 128, S], BF16, kind="Internal").ap()
    stash = nc.dram_tensor("stash", [NCH, 128, 1024], F32, kind="Internal").ap()
    gsc = nc.dram_tensor("gsc", [12, 128], F32, kind="Internal").ap()

    dbg_t = {}

    def dbg_ap(name, shape, dt=F32):
        dbg_t[name] = nc.dram_tensor("dbg_" + name, shape, dt, kind="ExternalOutput").ap()
        return dbg_t[name]

    with ExitStack() as ctx:
        big = ctx.enter_context(nc.sbuf_tensor("arena", [128, ARENA_F32], F32))
        A = Arena(big[:])
        PSA = ctx.enter_context(nc.psum_tensor("psa", [128, 2048], F32))
        PSB = ctx.enter_context(nc.psum_tensor("psb", [128, 2048], F32))

        def bank(k, off=0, n=512):
            t = PSA if k < 4 else PSB
            return t[:, 512 * (k % 4) + off: 512 * (k % 4) + off + n]

        def bankspan(k, nb):
            t = PSA if k < 4 else PSB
            return t[:, 512 * (k % 4): 512 * (k % 4) + 512 * nb]

        block = ctx.enter_context(nc.Block())
        _sems = {}

        def sem(name):
            if name not in _sems:
                _sems[name] = Sem(nc, ctx, name)
            return _sems[name]

        vch = sem("vch")
        sch = sem("sch")

        def vbar(vector, instr):
            vch.inc(instr)
            vector.wait_ge(vch.h, vch.v)

        def sbar(scalar, instr):
            sch.inc(instr)
            scalar.wait_ge(sch.h, sch.v)

        r_cos = A.alloc_b16(S, "cos")
        r_sin = A.alloc_b16(S, "sinx")
        r_id = A.alloc_b16(128, "ident")
        r_cols = A.alloc(16, "lncols")
        r_gsum = A.alloc(8, "gsum")
        r_tq = A.alloc(8, "tq")
        r_const = A.alloc(8, "const")
        r_trash = A.alloc_b16(1024, "trash")
        nc.const_aps.register(F32, -0.5, r_const.f32[:, 0:1])
        identb = r_id.b16
        trashA = r_trash.b16[:, 0:1024]

        s_fix = sem("fix")

        @block.sync
        def _(sync):
            s_fix.inc(sync.dma_start(out=r_cos.b16, in_=cosf), 16)
            s_fix.inc(sync.dma_start(out=r_sin.b16, in_=sinx), 16)
            s_fix.inc(sync.dma_start(out=r_id.b16, in_=ident), 16)

        @block.vector
        def _(vector):
            vector.memset(r_const.f32[:, 0:1], -0.5)
            i2 = vector.memset(r_gsum.f32, 0.0)
            vbar(vector, i2)
            vector.wait_ge(s_fix.h, s_fix.v)

        r_gs = [A.alloc(GSTRIP, "gs0"), A.alloc(GSTRIP, "gs1")]
        gflat = [w1b.rearrange("a b c -> (a b c)"), w2b.rearrange("a b c -> (a b c)"),
                 w3b.rearrange("a b c d -> (a b c d)")]
        s_gld = [sem("gld0"), sem("gld1")]
        s_gvs = sem("gvs")
        _gi = [0]

        def emit_gamma(n):
            j0 = _gi[0]
            n = min(n, NGTOT - j0)
            if n <= 0:
                return
            _gi[0] += n

            @block.sync
            def _(sync):
                for j in range(j0, j0 + n):
                    pp = j % 2
                    if j >= 2:
                        sync.wait_ge(s_gvs.h, 2 * (j - 1))
                    w, js = j // NGS, j % NGS
                    src = gflat[w][128 * GSTRIP * js: 128 * GSTRIP * (js + 1)]
                    s_gld[pp].inc(sync.dma_start(
                        out=r_gs[pp].f32, in_=src.rearrange("(p f) -> p f", f=GSTRIP)), 16)

            @block.vector
            def _(vector):
                for j in range(j0, j0 + n):
                    pp = j % 2
                    w = j // NGS
                    vector.wait_ge(s_gld[pp].h, 16 * (j // 2 + 1))
                    i1 = vector.reduce_sum(r_gsum.f32[:, 4 + pp:5 + pp], r_gs[pp].f32,
                                           axis=AXL.X, apply_absolute_value=True)
                    s_gvs.inc(i1)
                    vector.wait_ge(s_gvs.h, s_gvs.v)
                    i2 = vector.tensor_tensor(r_gsum.f32[:, w:w + 1], r_gsum.f32[:, w:w + 1],
                                              r_gsum.f32[:, 4 + pp:5 + pp], OP.add)
                    s_gvs.inc(i2)
                    vector.wait_ge(s_gvs.h, s_gvs.v)

        def ln_cols(s_v, s_s, nfeat, rscale=None):
            @block.vector
            def _(vector):
                cc = r_cols.f32
                vector.wait_ge(s_v.h, s_v.v)
                vector.tensor_scalar(cc[:, 2:3], cc[:, 0:1], 1.0 / nfeat, None, OP.mult)
                i1 = vector.tensor_scalar(cc[:, 3:4], cc[:, 1:2], 1.0 / nfeat, None, OP.mult)
                vbar(vector, i1)
                i2 = vector.tensor_tensor(cc[:, 6:7], cc[:, 2:3], cc[:, 2:3], OP.mult)
                vbar(vector, i2)
                i3 = vector.tensor_tensor(cc[:, 3:4], cc[:, 3:4], cc[:, 6:7], OP.subtract)
                vbar(vector, i3)
                i4 = vector.tensor_scalar(cc[:, 3:4], cc[:, 3:4], EPS_LN, None, OP.add)
                s_v.inc(i4)

            @block.scalar
            def _(scalar):
                cc = r_cols.f32
                scalar.wait_ge(s_v.h, s_v.v)
                i1 = scalar.activation(cc[:, 4:5], cc[:, 3:4], AF.Ln)
                sbar(scalar, i1)
                i2 = scalar.activation(cc[:, 4:5], cc[:, 4:5], AF.Exp, scale=-0.5)
                s_s.inc(i2)

            @block.vector
            def _(vector):
                cc = r_cols.f32
                vector.wait_ge(s_s.h, s_s.v)
                if rscale is not None:
                    ir = vector.tensor_scalar(cc[:, 4:5], cc[:, 4:5], rscale, None, OP.mult)
                    vbar(vector, ir)
                i5 = vector.scalar_tensor_tensor(cc[:, 5:6], cc[:, 2:3], -1.0, cc[:, 4:5],
                                                 OP.mult, OP.mult)
                s_v.inc(i5)
                vector.wait_ge(s_v.h, s_v.v)

        r_xn1t = A.alloc_b16(16 * S, "xn1t")
        xn1t3 = r_xn1t.b16.rearrange("p (dc t) -> p dc t", t=S)
        r_x0 = A.alloc(D, "x0buf")
        r_stg = A.alloc_b16(D, "xstage")
        s_x0 = sem("x0ld")
        s_p0v = sem("p0v")
        s_p0s = sem("p0s")
        s_p0t = sem("p0t")
        s_p0e = sem("p0e")

        @block.sync
        def _(sync):
            s_x0.inc(sync.dma_start(out=r_x0.f32, in_=x0[0:128, :]), 16)

        for i in range(p0_chunks):
            @block.scalar
            def _(scalar, i=i):
                scalar.wait_ge(s_x0.h, 16 * (i + 1))
                if i > 0:
                    scalar.wait_ge(s_p0v.h, s_p0v.v)
                    scalar.wait_ge(s_p0t.h, s_p0t.v)
                scalar.activation(trashA, r_x0.f32[:, 0:1024], AF.Square)
                i2 = scalar.activation(r_stg.b16[:, 0:1024], r_x0.f32[:, 1024:2048],
                                       AF.Square)
                s_p0s.inc(i2)

            @block.vector
            def _(vector, i=i):
                vector.wait_ge(s_x0.h, 16 * (i + 1))
                vector.reduce_sum(r_cols.f32[:, 0:1], r_x0.f32, axis=AXL.X)
                vector.wait_ge(s_p0s.h, s_p0s.v)
                vector.reduce_sum(r_cols.f32[:, 6:7], trashA, axis=AXL.X)
                i3 = vector.reduce_sum(r_cols.f32[:, 1:2], r_stg.b16[:, 0:1024], axis=AXL.X)
                vbar(vector, i3)
                i4 = vector.tensor_tensor(r_cols.f32[:, 1:2], r_cols.f32[:, 1:2],
                                          r_cols.f32[:, 6:7], OP.add)
                s_p0v.inc(i4)

            ln_cols(s_p0v, s_p0s, D)

            @block.vector
            def _(vector, i=i):
                cc = r_cols.f32
                if i > 0:
                    vector.wait_ge(s_p0t.h, s_p0t.v)
                i1 = vector.tensor_scalar(r_stg.b16, r_x0.f32, cc[:, 4:5], cc[:, 5:6],
                                          OP.mult, OP.add)
                s_p0v.inc(i1)

            if i + 1 < NCH:
                @block.sync
                def _(sync, i=i):
                    sync.wait_ge(s_p0v.h, s_p0v.v)
                    sync.wait_ge(s_p0s.h, s_p0s.v)
                    s_x0.inc(sync.dma_start(out=r_x0.f32,
                                            in_=x0[128 * (i + 1):128 * (i + 2), :]), 16)

            for g in range(4):
                bk = g % 2

                @block.tensor
                def _(tensor, i=i, g=g, bk=bk):
                    tensor.wait_ge(s_p0v.h, s_p0v.v)
                    tensor.wait_ge(s_fix.h, s_fix.v)
                    gidx = i * 4 + g
                    if gidx >= 2:
                        tensor.wait_ge(s_p0e.h, gidx - 1)
                    for d4 in range(4):
                        d = 4 * g + d4
                        mm = tensor.matmul(bank(bk, 128 * d4, 128),
                                           r_stg.b16[:, 128 * d:128 * (d + 1)],
                                           identb, start=(d4 == 0), stop=(d4 == 3))
                    s_p0t.inc(mm)

                @block.vector
                def _(vector, i=i, g=g, bk=bk):
                    vector.wait_ge(s_p0t.h, s_p0t.v)
                    cp = vector.tensor_copy(
                        xn1t3[:, 4 * g:4 * g + 4, 128 * i:128 * (i + 1)],
                        bankspan(bk, 1).rearrange("p (a b) -> p a b", b=128))
                    s_p0e.inc(cp)

            if gamma_on:
                emit_gamma(4)

        if "xn1t" in dbg:
            s_d0 = sem("d0")

            @block.scalar
            def _(scalar):
                scalar.wait_ge(s_p0e.h, s_p0e.v)
                s_d0.inc(scalar.dma_start(out=dbg_ap("xn1t", [128, 16 * S], BF16),
                                          in_=r_xn1t.b16), 16)

        if live("p1"):
            r_x0.free()
            r_stg.free()
            r_qf = A.alloc_b16(H * S, "qf")
            r_kf = A.alloc_b16(KVH * S, "kf")
            r_vf = A.alloc_b16(KVH * S, "vf")
            r_wsl = A.alloc(D, "wslab")
            r_wb = [A.alloc_b16(D, "wb0"), A.alloc_b16(D, "wb1")]
            r_pst = A.alloc_b16(D, "p1stage")
            r_sh = A.alloc_b16(D, "p1shift")
            r_ra = A.alloc_b16(D, "p1a")
            r_rb = A.alloc_b16(D, "p1b")

            s_wld = sem("wld")
            s_wc = sem("wc")
            s_p1t = sem("p1t")
            s_p1c = sem("p1c")
            s_sh = sem("shd")
            s_rva = sem("rva")
            s_rse = sem("rse")
            s_rvq = sem("rvq")

            ROPE_OCS = 20

            for oc in range(NO):
                ri = min(oc, ROPE_OCS)

                @block.sync
                def _(sync, oc=oc):
                    if oc == 0:
                        sync.wait_ge(s_p0e.h, s_p0e.v)
                        sync.wait_ge(s_p0v.h, s_p0v.v)
                        sync.wait_ge(s_p0s.h, s_p0s.v)
                    else:
                        sync.wait_ge(s_wc.h, oc)
                    s_wld.inc(sync.dma_start(
                        out=r_wsl.f32.rearrange("p (a b) -> p a b", b=128),
                        in_=wqkv[oc].rearrange("a p b -> p a b")), 16)

                @block.scalar
                def _(scalar, oc=oc):
                    scalar.wait_ge(s_wld.h, 16 * (oc + 1))
                    if oc >= 2:
                        scalar.wait_ge(s_p1t.h, oc - 1)
                    s_wc.inc(scalar.copy(r_wb[oc % 2].b16, r_wsl.f32))

                @block.tensor
                def _(tensor, oc=oc):
                    tensor.wait_ge(s_wc.h, oc + 1)
                    if oc == 0:
                        tensor.wait_ge(s_p0e.h, s_p0e.v)
                    if oc >= 2:
                        tensor.wait_ge(s_p1c.h, oc - 1)
                    pb = 0 if oc % 2 == 0 else 4
                    for d in range(16):
                        for tb in range(4):
                            mm = tensor.matmul(
                                bank(pb + tb),
                                r_wb[oc % 2].b16[:, 128 * d:128 * (d + 1)],
                                r_xn1t.b16[:, 2048 * d + 512 * tb: 2048 * d + 512 * (tb + 1)],
                                start=(d == 0), stop=(d == 15))
                    s_p1t.inc(mm)

                @block.scalar
                def _(scalar, oc=oc, ri=ri):
                    scalar.wait_ge(s_p1t.h, oc + 1)
                    pb = 0 if oc % 2 == 0 else 4
                    if oc < ROPE_OCS:
                        if ri > 0:
                            scalar.wait_ge(s_rva.h, 2 * ri)
                            scalar.wait_ge(s_sh.h, 32 * ri)
                        cp = scalar.copy(r_pst.b16, bankspan(pb, 4))
                        s_p1c.inc(cp)
                        scalar.wait_ge(s_p1c.h, s_p1c.v)
                        if ri > 0:
                            scalar.wait_ge(s_rvq.h, ri)
                        s_sh.inc(scalar.dma_start(out=r_sh.b16[0:64, :],
                                                  in_=r_pst.b16[64:128, :]), 16)
                        s_sh.inc(scalar.dma_start(out=r_sh.b16[64:128, :],
                                                  in_=r_pst.b16[0:64, :]), 16)
                    else:
                        g = oc - 20
                        s_p1c.inc(scalar.copy(r_vf.b16[:, 2048 * g:2048 * (g + 1)],
                                              bankspan(pb, 4)))

                if oc < ROPE_OCS:
                    @block.vector
                    def _(vector, oc=oc, ri=ri):
                        vector.wait_ge(s_p1c.h, oc + 1)
                        if ri > 0:
                            vector.wait_ge(s_rvq.h, ri)
                        ia = vector.tensor_tensor(r_ra.b16, r_pst.b16, r_cos.b16, OP.mult)
                        s_rva.inc(ia)
                        vector.wait_ge(s_sh.h, 32 * (ri + 1))
                        ib = vector.tensor_tensor(r_rb.b16, r_sh.b16, r_sin.b16, OP.mult)
                        vbar(vector, ib)
                        ir = vector.tensor_tensor(r_ra.b16, r_ra.b16, r_rb.b16, OP.add)
                        vbar(vector, ir)
                        im = vector.tensor_scalar(r_rb.b16, r_ra.b16, 0.0, None, OP.min)
                        s_rva.inc(im)

                    @block.scalar
                    def _(scalar, ri=ri):
                        scalar.wait_ge(s_rva.h, 2 * ri + 2)
                        s_rse.inc(scalar.activation(r_rb.b16, r_rb.b16, AF.Exp))

                    @block.vector
                    def _(vector, oc=oc, ri=ri):
                        vector.wait_ge(s_rse.h, ri + 1)
                        if oc < 16:
                            dst = r_qf.b16[:, 2048 * oc:2048 * (oc + 1)]
                        else:
                            dst = r_kf.b16[:, 2048 * (oc - 16):2048 * (oc - 15)]
                        s_rvq.inc(vector.scalar_tensor_tensor(dst, r_ra.b16, 0.0, r_rb.b16,
                                                              OP.max, OP.add))
                if gamma_on:
                    emit_gamma(8)

            if "qf" in dbg:
                s_d1 = sem("d1")

                @block.scalar
                def _(scalar):
                    scalar.wait_ge(s_rvq.h, s_rvq.v)
                    scalar.wait_ge(s_p1c.h, s_p1c.v)
                    s_d1.inc(scalar.dma_start(out=dbg_ap("qf", [128, H * S], BF16),
                                              in_=r_qf.b16), 16)
                    s_d1.inc(scalar.dma_start(out=dbg_ap("kf", [128, KVH * S], BF16),
                                              in_=r_kf.b16), 16)
                    s_d1.inc(scalar.dma_start(out=dbg_ap("vf", [128, KVH * S], BF16),
                                              in_=r_vf.b16), 16)

            for r in (r_wsl, r_wb[0], r_wb[1], r_pst, r_sh, r_ra, r_rb, r_xn1t,
                      r_cos, r_sin):
                r.free()

        if live("p2a"):
            r_attb = A.alloc_b16(NCH * 2048, "attb")
            r_mask = A.alloc_b16(512, "mask4")
            r_ktk = A.alloc_b16(512, "ktok")
            r_vag = A.alloc_b16(4 * 132, "vaug")
            r_smt = A.alloc_b16(512, "smt")
            r_sta = A.alloc_b16(4 * 132, "stateaug")
            r_rec = A.alloc(8, "recden")

            s_mld = sem("mld")
            s2t = sem("s2t")
            s2v = sem("s2v")
            p1_c_done0 = s_p1c.v
            p1_q_done0 = s_rvq.v
            p1_sh_done0 = s_sh.v
            p1_wc_done0 = s_wc.v

            @block.sync
            def _(sync):
                sync.wait_ge(s_rvq.h, p1_q_done0)
                sync.wait_ge(s_p1c.h, p1_c_done0)
                sync.wait_ge(s_sh.h, p1_sh_done0)
                sync.wait_ge(s_wc.h, p1_wc_done0)
                s_mld.inc(sync.dma_start(out=r_mask.b16, in_=mask4), 16)

            @block.vector
            def _(vector):
                vector.wait_ge(s_sh.h, p1_sh_done0)
                m0 = vector.memset(r_vag.f32, 0.0)
                vbar(vector, m0)
                for g in range(4):
                    mi = vector.memset(r_vag.b16[:, 132 * g + 128:132 * g + 129], 1.0)
                s2v.inc(mi)
            v_vag_ones = s2v.v

            @block.tensor
            def _(tensor):
                tensor.wait_ge(s_fix.h, s_fix.v)
                tensor.matmul(bank(0, 500, 1), identb, identb[:, 0:1], start=True, stop=True)
                s2t.inc(tensor.matmul(bank(1, 500, 1), identb, identb[:, 0:1],
                                      start=True, stop=True))

            p1_c_done = s_p1c.v
            p1_q_done = s_rvq.v

            t_trans = {}
            v_tevac = {}
            t_scores = {}
            v_mask = {}
            t_att = {}
            v_atte = {}
            t_supd = [0]
            v_sevac = [0]

            for i in range(NCH):
                @block.tensor
                def _(tensor, i=i):
                    if i == 0:
                        tensor.wait_ge(s_rvq.h, p1_q_done)
                        tensor.wait_ge(s_p1c.h, p1_c_done)
                        tensor.wait_ge(s2v.h, v_vag_ones)
                    else:
                        tensor.wait_ge(s2v.h, v_mask[(i - 1, 3)])
                    for g in range(4):
                        mm = tensor.matmul(
                            bank(2, 128 * g, 128),
                            r_kf.b16[:, 2048 * g + 128 * i:2048 * g + 128 * (i + 1)],
                            identb, start=(g == 0), stop=(g == 3))
                    for g in range(4):
                        mm = tensor.matmul(
                            bank(3, 128 * g, 128),
                            r_vf.b16[:, 2048 * g + 128 * i:2048 * g + 128 * (i + 1)],
                            identb, start=(g == 0), stop=(g == 3))
                    t_trans[i] = s2t.inc(mm)

                @block.vector
                def _(vector, i=i):
                    vector.wait_ge(s2t.h, t_trans[i])
                    if i == 0:
                        vector.wait_ge(s_mld.h, 16)
                    vector.tensor_copy(r_ktk.b16, bankspan(2, 1))
                    cp = vector.tensor_copy(
                        r_vag.b16.rearrange("p (a b) -> p a b", b=132)[:, :, 0:128],
                        bankspan(3, 1).rearrange("p (a b) -> p a b", b=128))
                    s2v.inc(cp)
                    v_tevac[i] = s2v.v

                for g in range(4):
                    sb = 2 + (g % 2)

                    @block.tensor
                    def _(tensor, i=i, g=g, sb=sb):
                        tensor.wait_ge(s2v.h, v_tevac[i])
                        if g >= 2:
                            tensor.wait_ge(s2v.h, v_mask[(i, g - 2)])
                        for j in range(4):
                            h = 4 * g + j
                            mm = tensor.matmul(
                                bank(sb, 128 * j, 128),
                                r_kf.b16[:, 2048 * g + 128 * i:2048 * g + 128 * (i + 1)],
                                r_qf.b16[:, 2048 * h + 128 * i:2048 * h + 128 * (i + 1)],
                                start=(j == 0), stop=(j == 3))
                        t_scores[(i, g)] = s2t.inc(mm)

                    @block.vector
                    def _(vector, i=i, g=g, sb=sb):
                        vector.wait_ge(s2t.h, t_scores[(i, g)])
                        if g > 0:
                            vector.wait_ge(s2t.h, t_att[(i, 4 * g - 1)])
                        elif i > 0:
                            vector.wait_ge(s2t.h, t_att[(i - 1, 15)])
                        s2v.inc(vector.tensor_tensor(r_smt.b16, bankspan(sb, 1),
                                                     r_mask.b16, OP.mult))
                        v_mask[(i, g)] = s2v.v

                    for j in range(4):
                        h = 4 * g + j
                        ab = 4 + (h % 2)

                        @block.tensor
                        def _(tensor, i=i, g=g, j=j, h=h, ab=ab):
                            tensor.wait_ge(s2v.h, v_mask[(i, g)])
                            if h >= 2:
                                tensor.wait_ge(s2v.h, v_atte[(i, h - 2)])
                            elif i > 0:
                                tensor.wait_ge(s2v.h, v_atte[(i - 1, 14 + h)])
                            mm = tensor.matmul(bank(ab, 0, 129),
                                               r_smt.b16[:, 128 * j:128 * (j + 1)],
                                               r_vag.b16[:, 132 * g:132 * g + 129],
                                               start=True, stop=(i == 0))
                            if i > 0:
                                mm = tensor.matmul(
                                    bank(ab, 0, 129),
                                    r_qf.b16[:, 2048 * h + 128 * i:2048 * h + 128 * (i + 1)],
                                    r_sta.b16[:, 132 * g:132 * g + 129],
                                    start=False, stop=True)
                            t_att[(i, h)] = s2t.inc(mm)

                        @block.vector
                        def _(vector, i=i, h=h, ab=ab):
                            vector.wait_ge(s2t.h, t_att[(i, h)])
                            cc = r_rec.f32[:, (h % 2):(h % 2) + 1]
                            i1 = vector.tensor_scalar(cc, bank(ab, 128, 1), EPS_ATT,
                                                      None, OP.add)
                            vbar(vector, i1)
                            i2 = vector.reciprocal(cc, cc)
                            vbar(vector, i2)
                            i3 = vector.tensor_scalar(
                                r_attb.b16[:, 2048 * i + 128 * h:2048 * i + 128 * (h + 1)],
                                bank(ab, 0, 128), cc, None, OP.mult)
                            s2v.inc(i3)
                            v_atte[(i, h)] = s2v.v

                if i < NCH - 1:
                    @block.tensor
                    def _(tensor, i=i):
                        if i > 0:
                            tensor.wait_ge(s2v.h, v_sevac[0])
                        for g in range(4):
                            mm = tensor.matmul(bank(g // 2, 140 * (g % 2), 129),
                                               r_ktk.b16[:, 128 * g:128 * (g + 1)],
                                               r_vag.b16[:, 132 * g:132 * g + 129],
                                               start=False, stop=False,
                                               skip_group_check=True)
                        t_supd[0] = s2t.inc(mm)

                    @block.vector
                    def _(vector, i=i):
                        vector.wait_ge(s2t.h, t_supd[0])
                        for g in range(4):
                            cp = vector.tensor_copy(r_sta.b16[:, 132 * g:132 * g + 129],
                                                    bank(g // 2, 140 * (g % 2), 129))
                        v_sevac[0] = s2v.inc(cp)

                if gamma_on:
                    emit_gamma(11)

            if "attb" in dbg:
                s_d2 = sem("d2")

                @block.scalar
                def _(scalar):
                    scalar.wait_ge(s2v.h, s2v.v)
                    s_d2.inc(scalar.dma_start(
                        out=dbg_ap("attb", [S, H * HD], BF16).rearrange(
                            "(i p) o -> p i o", p=128),
                        in_=r_attb.b16.rearrange("p (i o) -> p i o", o=2048)), 16)

            for r in (r_ktk, r_vag, r_smt, r_sta, r_rec, r_mask, r_qf, r_kf, r_vf):
                r.free()
            p2a_v_final = s2v.v

        if live("p2b"):
            r_att_t = [A.alloc_b16(2048, "attT0"), A.alloc_b16(2048, "attT1")]
            r_wotb = A.alloc_b16(16 * 2048, "wotb")
            r_wosl = A.alloc(D, "woslab")
            r_x0b = A.alloc(D, "x0re")
            r_x1b = A.alloc(D, "x1buf")

            s_wold = sem("wold")
            s_woc = sem("woc")
            s_x0b = sem("x0re")
            s_x1w = sem("x1w")
            s2bt = sem("s2bt")
            s2bv = sem("s2bv")
            s2bs = sem("s2bs")

            @block.sync
            def _(sync):
                sync.wait_ge(s2v.h, p2a_v_final)
                s_x0b.inc(sync.dma_start(out=r_x0b.f32, in_=x0[0:128, :]), 16)

            for oc in range(16):
                @block.sync
                def _(sync, oc=oc):
                    if oc > 0:
                        sync.wait_ge(s_woc.h, oc)
                    s_wold.inc(sync.dma_start(out=r_wosl.f32,
                                              in_=wo[128 * oc:128 * (oc + 1), :]), 16)

                @block.scalar
                def _(scalar, oc=oc):
                    scalar.wait_ge(s_wold.h, 16 * (oc + 1))
                    s_woc.inc(scalar.copy(r_wotb.b16[:, 2048 * oc:2048 * (oc + 1)],
                                          r_wosl.f32))

            v_ate = {}
            t_wo = {}
            v_x1ps = {}
            v_resid = {}
            v_xn2 = {}

            for i in range(NCH):
                for g in range(4):
                    ab = 4 + (g % 2)

                    @block.tensor
                    def _(tensor, i=i, g=g, ab=ab):
                        if i == 0 and g == 0:
                            tensor.wait_ge(s2v.h, p2a_v_final)
                        gid = 4 * i + g
                        if gid >= 2:
                            prev = (i, g - 2) if g >= 2 else (i - 1, g + 2)
                            tensor.wait_ge(s2bv.h, v_ate[prev])
                        for d4 in range(4):
                            oc = 4 * g + d4
                            mm = tensor.matmul(
                                bank(ab, 128 * d4, 128),
                                r_attb.b16[:, 2048 * i + 128 * oc:2048 * i + 128 * (oc + 1)],
                                identb, start=(d4 == 0), stop=(d4 == 3))
                        s2bt.inc(mm)

                    @block.vector
                    def _(vector, i=i, g=g, ab=ab):
                        vector.wait_ge(s2bt.h, s2bt.v)
                        att3 = r_att_t[i % 2].b16.rearrange("p (a b) -> p a b", b=128)
                        cp = vector.tensor_copy(
                            att3[:, 4 * g:4 * g + 4, :],
                            bankspan(ab, 1).rearrange("p (a b) -> p a b", b=128))
                        s2bv.inc(cp)
                        v_ate[(i, g)] = s2bv.v

                @block.tensor
                def _(tensor, i=i):
                    tensor.wait_ge(s2bv.h, v_ate[(i, 3)])
                    if i > 0:
                        tensor.wait_ge(s2bv.h, v_x1ps[i - 1])
                        tensor.wait_ge(s2bs.h, s2bs.v)
                    for oc in range(16):
                        if i == 0:
                            tensor.wait_ge(s_woc.h, oc + 1)
                        for b in range(4):
                            mm = tensor.matmul(
                                bank(b), r_att_t[i % 2].b16[:, 128 * oc:128 * (oc + 1)],
                                r_wotb.b16[:, 2048 * oc + 512 * b:2048 * oc + 512 * (b + 1)],
                                start=(oc == 0), stop=(oc == 15))
                    t_wo[i] = s2bt.inc(mm)

                @block.vector
                def _(vector, i=i):
                    vector.wait_ge(s2bt.h, t_wo[i])
                    i1 = vector.reduce_sum(r_cols.f32[:, 0:1], PSA[:, :], axis=AXL.X)
                    s2bv.inc(i1)

                v_worad = s2bv.v

                @block.scalar
                def _(scalar, i=i, v_worad=v_worad):
                    scalar.wait_ge(s2bv.h, v_worad)
                    if i > 0:
                        scalar.wait_ge(s_x1w.h, 16 * i)
                    scalar.activation(trashA, PSA[:, 0:1024], AF.Square)
                    i2 = scalar.activation(r_x1b.b16[:, 0:1024], PSA[:, 1024:2048],
                                           AF.Square)
                    s2bs.inc(i2)

                @block.vector
                def _(vector, i=i):
                    vector.wait_ge(s2bs.h, s2bs.v)
                    vector.reduce_sum(r_cols.f32[:, 6:7], trashA, axis=AXL.X)
                    i3 = vector.reduce_sum(r_cols.f32[:, 1:2], r_x1b.b16[:, 0:1024],
                                           axis=AXL.X)
                    vbar(vector, i3)
                    i4 = vector.tensor_tensor(r_cols.f32[:, 1:2], r_cols.f32[:, 1:2],
                                              r_cols.f32[:, 6:7], OP.add)
                    s2bv.inc(i4)

                ln_cols(s2bv, s2bs, D)

                @block.vector
                def _(vector, i=i):
                    cc = r_cols.f32
                    if i > 0:
                        vector.wait_ge(s_x1w.h, 16 * i)
                        vector.wait_ge(s2bv.h, v_xn2[i - 1])
                    i1 = vector.tensor_scalar(r_x1b.f32, PSA[:, :], cc[:, 4:5], cc[:, 5:6],
                                              OP.mult, OP.add)
                    v_x1ps[i] = s2bv.inc(i1)
                    vector.wait_ge(s2bv.h, s2bv.v)
                    vector.wait_ge(s_x0b.h, 16 * (i + 1))
                    i2 = vector.tensor_tensor(r_x1b.f32, r_x1b.f32, r_x0b.f32, OP.add)
                    v_resid[i] = s2bv.inc(i2)
                    vector.wait_ge(s2bv.h, s2bv.v)
                    vector.reduce_sum(cc[:, 0:1], r_x1b.f32, axis=AXL.X)

                @block.scalar
                def _(scalar, i=i):
                    scalar.wait_ge(s2bv.h, v_resid[i])
                    s_x1w.inc(scalar.dma_start(out=x1s[128 * i:128 * (i + 1), :],
                                               in_=r_x1b.f32), 16)
                    scalar.activation(trashA, r_x1b.f32[:, 0:1024], AF.Square)
                    i2 = scalar.activation(r_attb.b16[:, 2048 * i:2048 * i + 1024],
                                           r_x1b.f32[:, 1024:2048], AF.Square)
                    s2bs.inc(i2)

                @block.vector
                def _(vector, i=i):
                    vector.wait_ge(s2bs.h, s2bs.v)
                    vector.reduce_sum(r_cols.f32[:, 6:7], trashA, axis=AXL.X)
                    i3 = vector.reduce_sum(r_cols.f32[:, 1:2],
                                           r_attb.b16[:, 2048 * i:2048 * i + 1024],
                                           axis=AXL.X)
                    vbar(vector, i3)
                    i4 = vector.tensor_tensor(r_cols.f32[:, 1:2], r_cols.f32[:, 1:2],
                                              r_cols.f32[:, 6:7], OP.add)
                    s2bv.inc(i4)

                ln_cols(s2bv, s2bs, D, rscale=0.5)

                @block.vector
                def _(vector, i=i):
                    cc = r_cols.f32
                    i1 = vector.tensor_scalar(r_attb.b16[:, 2048 * i:2048 * (i + 1)],
                                              r_x1b.f32, cc[:, 4:5], cc[:, 5:6],
                                              OP.mult, OP.add)
                    v_xn2[i] = s2bv.inc(i1)

                if i + 1 < NCH:
                    @block.sync
                    def _(sync, i=i):
                        sync.wait_ge(s2bv.h, v_resid[i])
                        s_x0b.inc(sync.dma_start(out=r_x0b.f32,
                                                 in_=x0[128 * (i + 1):128 * (i + 2), :]), 16)

                if gamma_on:
                    emit_gamma(6)

            if "x1" in dbg:
                s_d4 = sem("d4")

                @block.scalar
                def _(scalar):
                    scalar.wait_ge(s_x1w.h, s_x1w.v)
                    s_d4.inc(scalar.dma_start(out=dbg_ap("x1", [S, D]), in_=x1s), 16)

        if live("gamma"):
            emit_gamma(NGTOT)
            r_gtmp = A.alloc(256, "gtmp")
            s_gf = sem("gfin")

            @block.scalar
            def _(scalar):
                scalar.wait_ge(s_gvs.h, s_gvs.v)
                for w in range(3):
                    s_gf.inc(scalar.dma_start(out=gsc[w], in_=r_gsum.f32[:, w:w + 1]), 16)

            for w in range(3):
                @block.scalar
                def _(scalar, w=w):
                    scalar.wait_ge(s_gf.h, s_gf.v)
                    s_gf.inc(scalar.dma_start(out=r_gtmp.f32[0:1, 0:128],
                                              in_=gsc[w].rearrange("(a b) -> a b", a=1)), 16)

                @block.vector
                def _(vector, w=w):
                    vector.wait_ge(s_gf.h, s_gf.v)
                    cc = r_cols.f32
                    i1 = vector.reduce_sum(cc[0:1, 7:8], r_gtmp.f32[0:1, 0:128], axis=AXL.X)
                    vbar(vector, i1)
                    i2 = vector.tensor_scalar(cc[0:1, 7:8], cc[0:1, 7:8], 0.5 / NW,
                                              0.5 * EPS_Q, OP.mult, OP.add)
                    vbar(vector, i2)
                    i3 = vector.tensor_scalar(r_gtmp.f32[0:1, 0:128], r_gtmp.f32[0:1, 0:128],
                                              0.0, cc[0:1, 7:8], OP.mult, OP.add)
                    vbar(vector, i3)
                    i4 = vector.tensor_scalar(r_gtmp.f32[0:1, 128:256],
                                              r_gtmp.f32[0:1, 0:128], -1.0, None, OP.mult)
                    s_gf.inc(i4)

                @block.scalar
                def _(scalar, w=w):
                    scalar.wait_ge(s_gf.h, s_gf.v)
                    s_gf.inc(scalar.dma_start(out=gsc[4 + 2 * w],
                                              in_=r_gtmp.f32[0:1, 0:128]), 16)
                    s_gf.inc(scalar.dma_start(out=gsc[5 + 2 * w],
                                              in_=r_gtmp.f32[0:1, 128:256]), 16)
                    scalar.wait_ge(s_gf.h, s_gf.v)
                    s_gf.inc(scalar.dma_start(
                        out=r_tq.f32[:, 2 * w:2 * w + 1],
                        in_=gsc[4 + 2 * w].rearrange("(a b) -> a b", b=1)), 16)
                    s_gf.inc(scalar.dma_start(
                        out=r_tq.f32[:, 2 * w + 1:2 * w + 2],
                        in_=gsc[5 + 2 * w].rearrange("(a b) -> a b", b=1)), 16)
            gamma_ready = s_gf.v
            r_gtmp.free()
            r_gs[0].free()
            r_gs[1].free()

            if "tq" in dbg:
                s_d3 = sem("d3")

                @block.scalar
                def _(scalar):
                    scalar.wait_ge(s_gf.h, gamma_ready)
                    s_d3.inc(scalar.dma_start(out=dbg_ap("tq", [128, 8]), in_=r_tq.f32), 16)

        if live("xn2t"):
            r_att_t[0].free()
            r_att_t[1].free()
            r_wotb.free()
            r_wosl.free()
            r_x0b.free()
            r_x1b.free()
            r_xn2t = A.alloc_b16(16 * S, "xn2t")
            xn2t3 = r_xn2t.b16.rearrange("p (dc t) -> p dc t", t=S)
            s_xtt = sem("xtt")
            s_xte = sem("xte")

            for i in range(NCH):
                for g in range(4):
                    bk = 4 + (g % 2)

                    @block.tensor
                    def _(tensor, i=i, g=g, bk=bk):
                        if i == 0 and g == 0:
                            tensor.wait_ge(s2bv.h, v_xn2[NCH - 1])
                        gidx = i * 4 + g
                        if gidx >= 2:
                            tensor.wait_ge(s_xte.h, gidx - 1)
                        for d4 in range(4):
                            d = 4 * g + d4
                            mm = tensor.matmul(
                                bank(bk, 128 * d4, 128),
                                r_attb.b16[:, 2048 * i + 128 * d:2048 * i + 128 * (d + 1)],
                                identb, start=(d4 == 0), stop=(d4 == 3))
                        s_xtt.inc(mm)

                    @block.vector
                    def _(vector, i=i, g=g, bk=bk):
                        vector.wait_ge(s_xtt.h, s_xtt.v)
                        cp = vector.tensor_copy(
                            xn2t3[:, 4 * g:4 * g + 4, 128 * i:128 * (i + 1)],
                            bankspan(bk, 1).rearrange("p (a b) -> p a b", b=128))
                        s_xte.inc(cp)

            if "xn2t" in dbg:
                s_d5 = sem("d5")

                @block.scalar
                def _(scalar):
                    scalar.wait_ge(s_xte.h, s_xte.v)
                    s_d5.inc(scalar.dma_start(out=dbg_ap("xn2t", [128, 16 * S], BF16),
                                              in_=r_xn2t.b16), 16)

            r_attb.free()

        if live("p3"):
            r_q = [A.alloc_b16(2048, "q0"), A.alloc_b16(2048, "q1")]
            r_wstr = A.alloc(2048, "wstrip")
            r_sgb = A.alloc_b16(2048, "signb")
            r_w3q = [A.alloc_b16(1024, f"w3q{h}") for h in range(NHC)]
            r_w3s = A.alloc(1024, "w3strip")
            r_w3bb = A.alloc_b16(1024, "w3signb")
            r_hid = [A.alloc_b16(2048, "hid0"), A.alloc_b16(2048, "hid1")]
            r_sig = A.alloc(1024, "sig")

            s3ld = sem("s3ld")
            s3qs = sem("s3qs")
            s3qv = sem("s3qv")
            s3t = sem("s3t")
            s3s = sem("s3s")
            s3v = sem("s3v")
            s3h = [sem("s3h0"), sem("s3h1")]
            s3w3d = sem("s3w3d")
            s3w3s = sem("s3w3s")
            s3w3v = sem("s3w3v")

            def strip_done_tval(s):
                hc, k = s // 2, s % 2
                return 4 * hc + 2 * (k + 1)

            xte_final = s_xte.v
            x1w_final = s_x1w.v

            for s in range(2 * NHC):
                hc, k = s // 2, s % 2
                wsrc = w1b if k == 0 else w2b

                @block.sync
                def _(sync, s=s, hc=hc, wsrc=wsrc):
                    if s == 0:
                        sync.wait_ge(s_xte.h, xte_final)
                        sync.wait_ge(s_x1w.h, x1w_final)
                    else:
                        sync.wait_ge(s3qs.h, s)
                    s3ld.inc(sync.dma_start(
                        out=r_wstr.f32.rearrange("p (a b) -> p a b", b=128),
                        in_=wsrc[hc].rearrange("(a p) b -> p a b", p=128)), 16)

                @block.scalar
                def _(scalar, s=s, k=k):
                    scalar.wait_ge(s3ld.h, 16 * (s + 1))
                    if s == 0:
                        scalar.wait_ge(s_gf.h, gamma_ready)
                    if s >= 2:
                        scalar.wait_ge(s3t.h, strip_done_tval(s - 2))
                    if s >= 1:
                        scalar.wait_ge(s3qv.h, s)
                    w = k
                    scalar.activation(r_q[s % 2].b16, r_wstr.f32, AF.Sign,
                                      bias=r_tq.f32[:, 2 * w + 1:2 * w + 2])
                    i2 = scalar.activation(r_sgb.b16, r_wstr.f32, AF.Sign,
                                           bias=r_tq.f32[:, 2 * w:2 * w + 1])
                    s3qs.inc(i2)

                @block.vector
                def _(vector, s=s):
                    vector.wait_ge(s3qs.h, s + 1)
                    s3qv.inc(vector.tensor_tensor(r_q[s % 2].b16, r_q[s % 2].b16,
                                                  r_sgb.b16, OP.add))

                for th in range(2):
                    x1b0 = 0 if th == 0 else 4
                    x2b0 = 2 if th == 0 else 6
                    pb = x1b0 if k == 0 else x2b0

                    @block.tensor
                    def _(tensor, s=s, hc=hc, k=k, th=th, pb=pb):
                        tensor.wait_ge(s3qv.h, s + 1)
                        if s == 0 and th == 0:
                            tensor.wait_ge(s_xte.h, s_xte.v)
                        if k == 0 and hc > 0:
                            tensor.wait_ge(s3s.h, 2 * (hc - 1) + th + 1)
                        if k == 1 and hc > 0:
                            tensor.wait_ge(s3v.h, 2 * (hc - 1) + th + 1)
                        for d in range(16):
                            for b2 in range(2):
                                mm = tensor.matmul(
                                    bank(pb + b2),
                                    r_q[s % 2].b16[:, 128 * d:128 * (d + 1)],
                                    r_xn2t.b16[:, 2048 * d + 1024 * th + 512 * b2:
                                               2048 * d + 1024 * th + 512 * (b2 + 1)],
                                    start=(d == 0), stop=(d == 15))
                        s3t.inc(mm)

                if k == 1:
                    for th in range(2):
                        x1b0 = 0 if th == 0 else 4
                        x2b0 = 2 if th == 0 else 6
                        x1_done = 4 * hc + 1 + th
                        x2_done = 4 * hc + 3 + th

                        @block.scalar
                        def _(scalar, hc=hc, th=th, x1b0=x1b0, x1_done=x1_done):
                            scalar.wait_ge(s3t.h, x1_done)
                            if hc > 0 or th > 0:
                                scalar.wait_ge(s3v.h, 2 * hc + th)
                            s3s.inc(scalar.activation(r_sig.f32, bankspan(x1b0, 2),
                                                      AF.Sigmoid))

                        @block.vector
                        def _(vector, hc=hc, th=th, x2b0=x2b0, x2_done=x2_done):
                            vector.wait_ge(s3t.h, x2_done)
                            vector.wait_ge(s3s.h, 2 * hc + th + 1)
                            if hc >= 2:
                                vector.wait_ge(s3h[hc % 2].h, 16 * (hc // 2))
                            s3v.inc(vector.scalar_tensor_tensor(
                                r_hid[hc % 2].b16[:, 1024 * th:1024 * (th + 1)],
                                bankspan(x2b0, 2), 0.5, r_sig.f32, OP.mult, OP.mult))

                    @block.scalar
                    def _(scalar, hc=hc):
                        scalar.wait_ge(s3v.h, 2 * hc + 2)
                        s3h[hc % 2].inc(scalar.dma_start(out=hids[hc],
                                                         in_=r_hid[hc % 2].b16), 16)

                    @block.sync
                    def _(sync, hc=hc):
                        if hc == 0:
                            sync.wait_ge(s_xte.h, xte_final)
                            sync.wait_ge(s_x1w.h, x1w_final)
                        else:
                            sync.wait_ge(s3w3s.h, hc)
                        s3w3d.inc(sync.dma_start(out=r_w3s.f32, in_=w3b[0, hc]), 16)

                    @block.scalar
                    def _(scalar, hc=hc):
                        scalar.wait_ge(s3w3d.h, 16 * (hc + 1))
                        if hc >= 1:
                            scalar.wait_ge(s3w3v.h, hc)
                        scalar.activation(r_w3q[hc].b16, r_w3s.f32, AF.Sign,
                                          bias=r_tq.f32[:, 5:6])
                        s3w3s.inc(scalar.activation(r_w3bb.b16, r_w3s.f32, AF.Sign,
                                                    bias=r_tq.f32[:, 4:5]))

                    @block.vector
                    def _(vector, hc=hc):
                        vector.wait_ge(s3w3s.h, hc + 1)
                        s3w3v.inc(vector.tensor_tensor(r_w3q[hc].b16, r_w3q[hc].b16,
                                                       r_w3bb.b16, OP.add))

            if "q1s" in dbg:
                s_d6 = sem("d6")

                @block.scalar
                def _(scalar):
                    scalar.wait_ge(s3qv.h, 1)
                    s_d6.inc(scalar.dma_start(out=dbg_ap("q1s", [128, 2048], BF16),
                                              in_=r_q[0].b16), 16)

            if "hid" in dbg:
                s_d7 = sem("d7")

                @block.scalar
                def _(scalar):
                    scalar.wait_ge(s3h[0].h, s3h[0].v)
                    scalar.wait_ge(s3h[1].h, s3h[1].v)
                    s_d7.inc(scalar.dma_start(out=dbg_ap("hid", [NHC, 128, S], BF16),
                                              in_=hids), 16)

            for r in (r_q[0], r_q[1], r_wstr, r_sgb, r_hid[0], r_hid[1], r_sig, r_xn2t):
                r.free()
            p3_h_final = (s3h[0].v, s3h[1].v)
            p3_t_final = s3t.v

        if live("p4"):
            r_w3q2 = [A.alloc_b16(1024, f"w3q2_{h}") for h in range(NHC)]
            r_hsl = [A.alloc_b16(11 * 128, f"hsl{j}") for j in range(5)]
            r_stg4 = [A.alloc(1024, "p4stg0"), A.alloc(1024, "p4stg1")]
            r_p4st = A.alloc(32, "p4stats")

            s4ld = [sem(f"s4ld{j}") for j in range(5)]
            s4st = [sem("s4st0"), sem("s4st1")]
            s4x1 = sem("s4x1")
            s4t = sem("s4t")
            s4tq = sem("s4tq")
            s4v = sem("s4v")
            s4s = sem("s4s")
            s4w = [sem("s4w0"), sem("s4w1")]
            s4y = sem("s4y")
            s4w3d = sem("s4w3d")
            s4w3s = sem("s4w3s")
            s4w3v = sem("s4w3v")

            v4ps = {}

            def emit_slice_dmas(qgs, first_wait_p3=False):
                @block.sync
                def _(sync):
                    for qg in qgs:
                        rnd, rem = divmod(qg, 64)
                        t1, q = divmod(rem, 4)
                        if first_wait_p3 and qg == 0:
                            sync.wait_ge(s3h[0].h, p3_h_final[0])
                            sync.wait_ge(s3h[1].h, p3_h_final[1])
                        if qg >= 5:
                            sync.wait_ge(s4tq.h, qg - 4)
                        src = hids[11 * q:11 * (q + 1), :, 128 * t1:128 * (t1 + 1)]
                        s4ld[qg % 5].inc(sync.dma_start(
                            out=r_hsl[qg % 5].b16.rearrange("p (a b) -> p a b", b=128),
                            in_=src.rearrange("a p b -> p a b")), 16)

            emit_slice_dmas(range(0, 4), first_wait_p3=True)

            if "hsl0" in dbg:
                s_da = sem("da")

                @block.scalar
                def _(scalar):
                    scalar.wait_ge(s4ld[0].h, 16)
                    s_da.inc(scalar.dma_start(out=dbg_ap("hsl0", [128, 1408], BF16),
                                              in_=r_hsl[0].b16), 16)

            for rnd in range(p4_rounds):
                if rnd == 1:
                    r_w3s.free()
                    r_w3bb.free()
                    for _r in r_w3q:
                        _r.free()
                    r_stg4[0].free()
                    r_stg4[1].free()
                    r_sst = [A.alloc(1024, "rst0"), A.alloc(1024, "rst1")]
                    r_x1rd = A.alloc(2048, "x1rd")
                    r_yst = A.alloc(2048, "ystage")

                w3q_cur = r_w3q if rnd == 0 else r_w3q2

                for t1 in range(NCH):
                    ti = rnd * NCH + t1
                    pb = 0 if t1 % 2 == 0 else 2

                    if rnd == 1:
                        @block.sync
                        def _(sync, t1=t1):
                            if t1 == 0:
                                sync.wait_ge(s4tq.h, 4 * NCH)
                                sync.wait_ge(s4w[0].h, s4w[0].v)
                                sync.wait_ge(s4w[1].h, s4w[1].v)
                            if t1 >= 2:
                                sync.wait_ge(s4v.h, v4ps[NCH + t1 - 2])
                            s4st[t1 % 2].inc(sync.dma_start(out=r_sst[t1 % 2].f32,
                                                            in_=stash[t1]), 16)
                            if t1 >= 1:
                                sync.wait_ge(s4v.h, v4ps[NCH + t1 - 1] + 1)
                            s4x1.inc(sync.dma_start(out=r_x1rd.f32,
                                                    in_=x1s[128 * t1:128 * (t1 + 1), :]), 16)

                    @block.tensor
                    def _(tensor, rnd=rnd, t1=t1, ti=ti, pb=pb, w3q_cur=w3q_cur):
                        if ti >= 2:
                            tensor.wait_ge(s4v.h, v4ps[ti - 2])
                        if rnd == 0 and t1 == 0:
                            tensor.wait_ge(s3w3v.h, NHC)
                        if rnd == 1 and t1 == 0:
                            tensor.wait_ge(s4w3v.h, NHC)
                        for q in range(4):
                            qg = 64 * rnd + 4 * t1 + q
                            tensor.wait_ge(s4ld[qg % 5].h, 16 * (qg // 5 + 1))
                            buf = r_hsl[qg % 5]
                            if not (p4_parts & 4):
                                s4tq.inc(tensor.nop())
                                continue
                            for hh in range(11):
                                hc = 11 * q + hh
                                for b in range(2):
                                    mm = tensor.matmul(
                                        bank(pb + b),
                                        buf.b16[:, 128 * hh:128 * (hh + 1)],
                                        w3q_cur[hc].b16[:, 512 * b:512 * (b + 1)],
                                        start=(hc == 0), stop=(hc == NHC - 1))
                            s4tq.inc(mm)

                    nxt = 64 * rnd + 4 * (t1 + 1)
                    if rnd == 0 or t1 + 1 < NCH:
                        emit_slice_dmas(range(nxt, min(nxt + 4, 128)))

                    if rnd == 0 and not (p4_parts & 1):
                        @block.vector
                        def _(vector, t1=t1, ti=ti, pb=pb):
                            vector.wait_ge(s4tq.h, 4 * (ti + 1))
                            v4ps[ti] = s4v.inc(vector.memset(r_p4st.f32[:, 2 * t1:2 * t1 + 1], 0.0))
                    if rnd == 0 and (p4_parts & 1):
                        @block.vector
                        def _(vector, t1=t1, ti=ti, pb=pb):
                            vector.wait_ge(s4tq.h, 4 * (ti + 1))
                            if t1 >= 2:
                                vector.wait_ge(s4w[t1 % 2].h, 16 * (t1 // 2))
                            vector.tensor_copy(r_stg4[t1 % 2].f32, bankspan(pb, 2))
                            i2 = vector.reduce_sum(r_p4st.f32[:, 2 * t1:2 * t1 + 1],
                                                   bankspan(pb, 2), axis=AXL.X)
                            s4v.inc(i2)

                        sv_vrd = s4v.v

                        @block.scalar
                        def _(scalar, t1=t1, ti=ti, pb=pb, sv_vrd=sv_vrd):
                            scalar.wait_ge(s4v.h, sv_vrd)
                            s4s.inc(scalar.activation(trashA, bankspan(pb, 2), AF.Square))

                        @block.vector
                        def _(vector, t1=t1, ti=ti, pb=pb):
                            vector.wait_ge(s4s.h, s4s.v)
                            i3 = vector.reduce_sum(r_p4st.f32[:, 2 * t1 + 1:2 * t1 + 2],
                                                   trashA, axis=AXL.X)
                            v4ps[ti] = s4v.inc(i3)

                        @block.scalar
                        def _(scalar, t1=t1, ti=ti):
                            scalar.wait_ge(s4v.h, v4ps[ti])
                            s4w[t1 % 2].inc(scalar.dma_start(out=stash[t1],
                                                             in_=r_stg4[t1 % 2].f32), 16)


                        if t1 == 0 and "w3q0" in dbg:
                            s_d9 = sem("d9")

                            @block.scalar
                            def _(scalar):
                                scalar.wait_ge(s3w3v.h, NHC)
                                s_d9.inc(scalar.dma_start(
                                    out=dbg_ap("w3q0", [128, 1024], BF16),
                                    in_=r_w3q[0].b16), 16)
                                s_d9.inc(scalar.dma_start(
                                    out=dbg_ap("w3q7", [128, 1024], BF16),
                                    in_=r_w3q[7].b16), 16)

                        if t1 == NCH - 1 and "sth" in dbg:
                            s_d8 = sem("d8")

                            @block.scalar
                            def _(scalar):
                                scalar.wait_ge(s4w[0].h, s4w[0].v)
                                scalar.wait_ge(s4w[1].h, s4w[1].v)
                                s_d8.inc(scalar.dma_start(
                                    out=dbg_ap("sth", [NCH, 128, 1024]), in_=stash), 16)

                        for hc3 in (range(3 * t1, min(3 * (t1 + 1), NHC)) if (p4_parts & 2) else []):
                            @block.sync
                            def _(sync, hc3=hc3):
                                if hc3 == 0:
                                    sync.wait_ge(s3w3s.h, NHC)
                                else:
                                    sync.wait_ge(s4w3s.h, hc3)
                                s4w3d.inc(sync.dma_start(out=r_w3s.f32,
                                                         in_=w3b[1, hc3]), 16)

                            @block.scalar
                            def _(scalar, hc3=hc3):
                                scalar.wait_ge(s4w3d.h, 16 * (hc3 + 1))
                                if hc3 == 0:
                                    scalar.wait_ge(s3w3v.h, NHC)
                                    scalar.wait_ge(s3t.h, p3_t_final)
                                else:
                                    scalar.wait_ge(s4w3v.h, hc3)
                                scalar.activation(r_w3q2[hc3].b16, r_w3s.f32, AF.Sign,
                                                  bias=r_tq.f32[:, 5:6])
                                s4w3s.inc(scalar.activation(r_w3bb.b16, r_w3s.f32, AF.Sign,
                                                            bias=r_tq.f32[:, 4:5]))

                            @block.vector
                            def _(vector, hc3=hc3):
                                vector.wait_ge(s4w3s.h, hc3 + 1)
                                s4w3v.inc(vector.tensor_tensor(r_w3q2[hc3].b16,
                                                               r_w3q2[hc3].b16,
                                                               r_w3bb.b16, OP.add))
                    if rnd == 1:
                        @block.vector
                        def _(vector, t1=t1, ti=ti, pb=pb):
                            vector.wait_ge(s4tq.h, 4 * (ti + 1))
                            cc = r_cols.f32
                            i1 = vector.reduce_sum(cc[:, 0:1], bankspan(pb, 2), axis=AXL.X)
                            s4v.inc(i1)
                            vector.wait_ge(s4v.h, s4v.v)
                            vector.tensor_tensor(cc[:, 0:1], cc[:, 0:1],
                                                 r_p4st.f32[:, 2 * t1:2 * t1 + 1], OP.add)

                        sv_vrd1 = s4v.v

                        @block.scalar
                        def _(scalar, t1=t1, ti=ti, pb=pb, sv_vrd1=sv_vrd1):
                            scalar.wait_ge(s4v.h, sv_vrd1)
                            s4s.inc(scalar.activation(trashA, bankspan(pb, 2), AF.Square))

                        @block.vector
                        def _(vector, t1=t1, ti=ti, pb=pb):
                            cc = r_cols.f32
                            vector.wait_ge(s4s.h, s4s.v)
                            i3 = vector.reduce_sum(cc[:, 1:2], trashA, axis=AXL.X)
                            vbar(vector, i3)
                            i4 = vector.tensor_tensor(cc[:, 1:2], cc[:, 1:2],
                                                      r_p4st.f32[:, 2 * t1 + 1:2 * t1 + 2],
                                                      OP.add)
                            s4v.inc(i4)

                        ln_cols(s4v, s4s, D)

                        @block.vector
                        def _(vector, t1=t1, ti=ti, pb=pb):
                            cc = r_cols.f32
                            vector.wait_ge(s4st[t1 % 2].h, 16 * (t1 // 2 + 1))
                            vector.wait_ge(s4x1.h, 16 * (t1 + 1))
                            if t1 > 0:
                                vector.wait_ge(s4y.h, 16 * t1)
                            vector.tensor_scalar(r_yst.f32[:, 0:1024], r_sst[t1 % 2].f32,
                                                 cc[:, 4:5], cc[:, 5:6], OP.mult, OP.add)
                            i2 = vector.tensor_scalar(r_yst.f32[:, 1024:2048],
                                                      bankspan(pb, 2),
                                                      cc[:, 4:5], cc[:, 5:6],
                                                      OP.mult, OP.add)
                            v4ps[ti] = s4v.inc(i2)
                            vector.wait_ge(s4v.h, s4v.v)
                            i3 = vector.tensor_tensor(r_yst.f32, r_yst.f32, r_x1rd.f32,
                                                      OP.add)
                            s4v.inc(i3)

                        @block.scalar
                        def _(scalar, t1=t1):
                            scalar.wait_ge(s4v.h, s4v.v)
                            s4y.inc(scalar.dma_start(out=y[128 * t1:128 * (t1 + 1), :],
                                                     in_=r_yst.f32), 16)

        @block.sync
        def _(sync):
            for _s in _sems.values():
                if _s.v > 0:
                    sync.wait_ge(_s.h, _s.v)

    return nc


def _host_pack(Wq, Wk, Wv, Wo, w1, w2, w3):
    wqkvT = np.concatenate([np.asarray(Wq, np.float32), np.asarray(Wk, np.float32),
                            np.asarray(Wv, np.float32)], 0)
    wqkv_b = np.ascontiguousarray(
        wqkvT.T.reshape(16, 128, NO, 128).transpose(2, 0, 1, 3))
    wo_t = np.ascontiguousarray(np.asarray(Wo, np.float32).T)
    w1t = np.asarray(w1, np.float32).T
    w2t = np.asarray(w2, np.float32).T
    w1bb = np.ascontiguousarray(w1t.reshape(D, NHC, 128).transpose(1, 0, 2))
    w2bb = np.ascontiguousarray(w2t.reshape(D, NHC, 128).transpose(1, 0, 2))
    w3t = np.asarray(w3, np.float32).T
    w3bb = np.ascontiguousarray(w3t.reshape(NHC, 128, 2, 1024).transpose(2, 0, 1, 3))
    return wqkv_b, wo_t, w1bb, w2bb, w3bb


def _tables():
    inv = 1.0 / (10000.0 ** (np.arange(0, HD, 2, dtype=np.float64) / HD))
    t = np.arange(S, dtype=np.float64)
    fr = np.outer(t, inv)
    emb = np.concatenate([fr, fr], -1)
    cos = np.cos(emb).T.astype(BF)
    sin = np.sin(emb).T.astype(np.float32)
    sin[:64] *= -1.0
    sinx_t = sin.astype(BF)
    maskut = np.triu(np.ones((C, C), np.float32))
    mask4m = np.tile(maskut, (1, 4)).astype(BF)
    identm = np.eye(128, dtype=np.float32).astype(BF)
    return (np.ascontiguousarray(cos), np.ascontiguousarray(sinx_t),
            np.ascontiguousarray(mask4m), np.ascontiguousarray(identm))


def kernel(L0_attn_x, L0_attn_Wq, L0_attn_Wk, L0_attn_Wv, L0_attn_Wo,
           L0_ffn_w1, L0_ffn_w2, L0_ffn_w3):
    x = np.ascontiguousarray(np.asarray(L0_attn_x, np.float32))
    wqkv_b, wo_t, w1bb, w2bb, w3bb = _host_pack(L0_attn_Wq, L0_attn_Wk, L0_attn_Wv,
                                                L0_attn_Wo, L0_ffn_w1, L0_ffn_w2,
                                                L0_ffn_w3)
    cos, sinx_t, mask4m, identm = _tables()
    nc = build_nc()
    shared = dict(wqkv=wqkv_b, wo=wo_t, w1b=w1bb, w2b=w2bb, w3b=w3bb,
                  cosf=cos, sinx=sinx_t, mask4=mask4m, ident=identm)
    in_maps = [dict(x0=np.ascontiguousarray(x[c]), **shared) for c in range(8)]
    res = run_bass_kernel_spmd(nc, in_maps, core_ids=list(range(8)))
    return np.stack([np.asarray(r["y"], np.float32) for r in res.results], 0)


# version 02
# speedup vs baseline: 1.0136x; 1.0136x over previous
import numpy as np
import ml_dtypes
from contextlib import ExitStack

import concourse.bass as bass
import concourse.mybir as mybir
from concourse.bass_utils import run_bass_kernel_spmd
from concourse.alu_op_type import AluOpType as OP

F32 = mybir.dt.float32
BF16 = mybir.dt.bfloat16
AF = mybir.ActivationFunctionType
AXL = mybir.AxisListType
BF = ml_dtypes.bfloat16

S = 2048
D = 2048
H, KVH, HD = 16, 4, 128
NO = 24
C = 128
NCH = S // C
HFF = 5632
NHC = HFF // 128
EPS_LN = 1e-5
EPS_ATT = 1e-6
EPS_Q = 1e-8
NW = HFF * D
GSTRIP = 512
NGS = NW // (128 * GSTRIP)
NGTOT = 3 * NGS

ARENA_F32 = 52984

PHASES = ["p0", "p1", "p2a", "p2b", "gamma", "xn2t", "p3", "p4", "end"]


class Reg:
    def __init__(self, arena, off, cols, name):
        self.arena, self.off, self.cols, self.name = arena, off, cols, name

    @property
    def f32(self):
        return self.arena.ap[:, self.off:self.off + self.cols]

    @property
    def b16(self):
        return self.arena.ap[:, self.off:self.off + self.cols].bitcast(BF16)

    def free(self):
        self.arena.free(self)


class Arena:
    def __init__(self, ap):
        self.ap = ap
        self.avail = [(0, ARENA_F32)]

    def alloc(self, cols, name=""):
        cols = (cols + 7) & ~7
        for i, (o, c) in enumerate(self.avail):
            if c >= cols:
                self.avail[i] = (o + cols, c - cols)
                if self.avail[i][1] == 0:
                    self.avail.pop(i)
                return Reg(self, o, cols, name)
        raise RuntimeError(f"arena OOM for {name}: need {cols} f32 cols, free={self.avail}")

    def alloc_b16(self, cols_b16, name=""):
        return self.alloc((cols_b16 + 1) // 2, name)

    def free(self, reg):
        self.avail.append((reg.off, reg.cols))
        self.avail.sort()
        merged = []
        for o, c in self.avail:
            if merged and merged[-1][0] + merged[-1][1] == o:
                merged[-1] = (merged[-1][0], merged[-1][1] + c)
            else:
                merged.append((o, c))
        self.avail = merged


class Sem:
    def __init__(self, nc, ctx, name):
        self.h = ctx.enter_context(nc.semaphore(name))
        self.v = 0

    def inc(self, instr, n=1):
        instr.then_inc(self.h, n)
        self.v += n
        return self.v


def build_nc(dbg=(), stop_after=None, p0_chunks=NCH, gamma_on=True, p4_rounds=2, p4_parts=7):
    dbg = set(dbg)
    _sa = PHASES.index(stop_after) if stop_after else PHASES.index("end")

    def live(ph):
        return PHASES.index(ph) <= _sa

    nc = bass.Bass("TRN2", target_bir_lowering=False)

    x0 = nc.dram_tensor("x0", [S, D], F32, kind="ExternalInput").ap()
    wqkv = nc.dram_tensor("wqkv", [NO, 16, 128, 128], F32, kind="ExternalInput").ap()
    wo = nc.dram_tensor("wo", [D, D], F32, kind="ExternalInput").ap()
    w1b = nc.dram_tensor("w1b", [NHC, D, 128], F32, kind="ExternalInput").ap()
    w2b = nc.dram_tensor("w2b", [NHC, D, 128], F32, kind="ExternalInput").ap()
    w3b = nc.dram_tensor("w3b", [2, NHC, 128, 1024], F32, kind="ExternalInput").ap()
    cosf = nc.dram_tensor("cosf", [128, S], BF16, kind="ExternalInput").ap()
    sinx = nc.dram_tensor("sinx", [128, S], BF16, kind="ExternalInput").ap()
    mask4 = nc.dram_tensor("mask4", [128, 512], BF16, kind="ExternalInput").ap()
    ident = nc.dram_tensor("ident", [128, 128], BF16, kind="ExternalInput").ap()
    y = nc.dram_tensor("y", [S, D], F32, kind="ExternalOutput").ap()
    x1s = nc.dram_tensor("x1s", [S, D], F32, kind="Internal").ap()
    hids = nc.dram_tensor("hids", [NHC, 128, S], BF16, kind="Internal").ap()
    stash = nc.dram_tensor("stash", [NCH, 128, 1024], F32, kind="Internal").ap()
    gsc = nc.dram_tensor("gsc", [12, 128], F32, kind="Internal").ap()

    dbg_t = {}

    def dbg_ap(name, shape, dt=F32):
        dbg_t[name] = nc.dram_tensor("dbg_" + name, shape, dt, kind="ExternalOutput").ap()
        return dbg_t[name]

    with ExitStack() as ctx:
        big = ctx.enter_context(nc.sbuf_tensor("arena", [128, ARENA_F32], F32))
        A = Arena(big[:])
        PSA = ctx.enter_context(nc.psum_tensor("psa", [128, 2048], F32))
        PSB = ctx.enter_context(nc.psum_tensor("psb", [128, 2048], F32))

        def bank(k, off=0, n=512):
            t = PSA if k < 4 else PSB
            return t[:, 512 * (k % 4) + off: 512 * (k % 4) + off + n]

        def bankspan(k, nb):
            t = PSA if k < 4 else PSB
            return t[:, 512 * (k % 4): 512 * (k % 4) + 512 * nb]

        block = ctx.enter_context(nc.Block())
        _sems = {}

        def sem(name):
            if name not in _sems:
                _sems[name] = Sem(nc, ctx, name)
            return _sems[name]

        vch = sem("vch")
        sch = sem("sch")

        def vbar(vector, instr):
            vch.inc(instr)
            vector.wait_ge(vch.h, vch.v)

        def sbar(scalar, instr):
            sch.inc(instr)
            scalar.wait_ge(sch.h, sch.v)

        r_cos = A.alloc_b16(S, "cos")
        r_sin = A.alloc_b16(S, "sinx")
        r_id = A.alloc_b16(128, "ident")
        r_cols = A.alloc(16, "lncols")
        r_gsum = A.alloc(8, "gsum")
        r_tq = A.alloc(8, "tq")
        r_const = A.alloc(8, "const")
        r_trash = A.alloc_b16(1024, "trash")
        nc.const_aps.register(F32, -0.5, r_const.f32[:, 0:1])
        identb = r_id.b16
        trashA = r_trash.b16[:, 0:1024]

        s_fix = sem("fix")

        @block.sync
        def _(sync):
            s_fix.inc(sync.dma_start(out=r_cos.b16, in_=cosf), 16)
            s_fix.inc(sync.dma_start(out=r_sin.b16, in_=sinx), 16)
            s_fix.inc(sync.dma_start(out=r_id.b16, in_=ident), 16)

        @block.vector
        def _(vector):
            vector.memset(r_const.f32[:, 0:1], -0.5)
            i2 = vector.memset(r_gsum.f32, 0.0)
            vbar(vector, i2)
            vector.wait_ge(s_fix.h, s_fix.v)

        r_gs = [A.alloc(GSTRIP, "gs0"), A.alloc(GSTRIP, "gs1")]
        gflat = [w1b.rearrange("a b c -> (a b c)"), w2b.rearrange("a b c -> (a b c)"),
                 w3b.rearrange("a b c d -> (a b c d)")]
        s_gld = [sem("gld0"), sem("gld1")]
        s_gvs = sem("gvs")
        _gi = [0]

        def emit_gamma(n):
            j0 = _gi[0]
            n = min(n, NGTOT - j0)
            if n <= 0:
                return
            _gi[0] += n

            @block.sync
            def _(sync):
                for j in range(j0, j0 + n):
                    pp = j % 2
                    if j >= 2:
                        sync.wait_ge(s_gvs.h, 2 * (j - 1))
                    w, js = j // NGS, j % NGS
                    src = gflat[w][128 * GSTRIP * js: 128 * GSTRIP * (js + 1)]
                    s_gld[pp].inc(sync.dma_start(
                        out=r_gs[pp].f32, in_=src.rearrange("(p f) -> p f", f=GSTRIP)), 16)

            @block.vector
            def _(vector):
                for j in range(j0, j0 + n):
                    pp = j % 2
                    w = j // NGS
                    vector.wait_ge(s_gld[pp].h, 16 * (j // 2 + 1))
                    i1 = vector.reduce_sum(r_gsum.f32[:, 4 + pp:5 + pp], r_gs[pp].f32,
                                           axis=AXL.X, apply_absolute_value=True)
                    s_gvs.inc(i1)
                    vector.wait_ge(s_gvs.h, s_gvs.v)
                    i2 = vector.tensor_tensor(r_gsum.f32[:, w:w + 1], r_gsum.f32[:, w:w + 1],
                                              r_gsum.f32[:, 4 + pp:5 + pp], OP.add)
                    s_gvs.inc(i2)
                    vector.wait_ge(s_gvs.h, s_gvs.v)

        def ln_cols(s_v, s_s, nfeat, rscale=None):
            @block.vector
            def _(vector):
                cc = r_cols.f32
                vector.wait_ge(s_v.h, s_v.v)
                vector.tensor_scalar(cc[:, 2:3], cc[:, 0:1], 1.0 / nfeat, None, OP.mult)
                i1 = vector.tensor_scalar(cc[:, 3:4], cc[:, 1:2], 1.0 / nfeat, None, OP.mult)
                vbar(vector, i1)
                i2 = vector.tensor_tensor(cc[:, 6:7], cc[:, 2:3], cc[:, 2:3], OP.mult)
                vbar(vector, i2)
                i3 = vector.tensor_tensor(cc[:, 3:4], cc[:, 3:4], cc[:, 6:7], OP.subtract)
                vbar(vector, i3)
                i4 = vector.tensor_scalar(cc[:, 3:4], cc[:, 3:4], EPS_LN, None, OP.add)
                s_v.inc(i4)

            @block.scalar
            def _(scalar):
                cc = r_cols.f32
                scalar.wait_ge(s_v.h, s_v.v)
                i1 = scalar.activation(cc[:, 4:5], cc[:, 3:4], AF.Ln)
                sbar(scalar, i1)
                i2 = scalar.activation(cc[:, 4:5], cc[:, 4:5], AF.Exp, scale=-0.5)
                s_s.inc(i2)

            @block.vector
            def _(vector):
                cc = r_cols.f32
                vector.wait_ge(s_s.h, s_s.v)
                if rscale is not None:
                    ir = vector.tensor_scalar(cc[:, 4:5], cc[:, 4:5], rscale, None, OP.mult)
                    vbar(vector, ir)
                i5 = vector.scalar_tensor_tensor(cc[:, 5:6], cc[:, 2:3], -1.0, cc[:, 4:5],
                                                 OP.mult, OP.mult)
                s_v.inc(i5)
                vector.wait_ge(s_v.h, s_v.v)

        r_xn1t = A.alloc_b16(16 * S, "xn1t")
        xn1t3 = r_xn1t.b16.rearrange("p (dc t) -> p dc t", t=S)
        r_x0 = A.alloc(D, "x0buf")
        r_stg = A.alloc_b16(D, "xstage")
        s_x0 = sem("x0ld")
        s_p0v = sem("p0v")
        s_p0s = sem("p0s")
        s_p0t = sem("p0t")
        s_p0e = sem("p0e")

        @block.sync
        def _(sync):
            s_x0.inc(sync.dma_start(out=r_x0.f32, in_=x0[0:128, :]), 16)

        for i in range(p0_chunks):
            @block.scalar
            def _(scalar, i=i):
                scalar.wait_ge(s_x0.h, 16 * (i + 1))
                if i > 0:
                    scalar.wait_ge(s_p0v.h, s_p0v.v)
                    scalar.wait_ge(s_p0t.h, s_p0t.v)
                scalar.activation(trashA, r_x0.f32[:, 0:1024], AF.Square)
                i2 = scalar.activation(r_stg.b16[:, 0:1024], r_x0.f32[:, 1024:2048],
                                       AF.Square)
                s_p0s.inc(i2)

            @block.vector
            def _(vector, i=i):
                vector.wait_ge(s_x0.h, 16 * (i + 1))
                vector.reduce_sum(r_cols.f32[:, 0:1], r_x0.f32, axis=AXL.X)
                vector.wait_ge(s_p0s.h, s_p0s.v)
                vector.reduce_sum(r_cols.f32[:, 6:7], trashA, axis=AXL.X)
                i3 = vector.reduce_sum(r_cols.f32[:, 1:2], r_stg.b16[:, 0:1024], axis=AXL.X)
                vbar(vector, i3)
                i4 = vector.tensor_tensor(r_cols.f32[:, 1:2], r_cols.f32[:, 1:2],
                                          r_cols.f32[:, 6:7], OP.add)
                s_p0v.inc(i4)

            ln_cols(s_p0v, s_p0s, D)

            @block.vector
            def _(vector, i=i):
                cc = r_cols.f32
                if i > 0:
                    vector.wait_ge(s_p0t.h, s_p0t.v)
                i1 = vector.tensor_scalar(r_stg.b16, r_x0.f32, cc[:, 4:5], cc[:, 5:6],
                                          OP.mult, OP.add)
                s_p0v.inc(i1)

            if i + 1 < NCH:
                @block.sync
                def _(sync, i=i):
                    sync.wait_ge(s_p0v.h, s_p0v.v)
                    sync.wait_ge(s_p0s.h, s_p0s.v)
                    s_x0.inc(sync.dma_start(out=r_x0.f32,
                                            in_=x0[128 * (i + 1):128 * (i + 2), :]), 16)

            for g in range(4):
                bk = g % 2

                @block.tensor
                def _(tensor, i=i, g=g, bk=bk):
                    tensor.wait_ge(s_p0v.h, s_p0v.v)
                    tensor.wait_ge(s_fix.h, s_fix.v)
                    gidx = i * 4 + g
                    if gidx >= 2:
                        tensor.wait_ge(s_p0e.h, gidx - 1)
                    for d4 in range(4):
                        d = 4 * g + d4
                        mm = tensor.matmul(bank(bk, 128 * d4, 128),
                                           r_stg.b16[:, 128 * d:128 * (d + 1)],
                                           identb, start=(d4 == 0), stop=(d4 == 3))
                    s_p0t.inc(mm)

                @block.vector
                def _(vector, i=i, g=g, bk=bk):
                    vector.wait_ge(s_p0t.h, s_p0t.v)
                    cp = vector.tensor_copy(
                        xn1t3[:, 4 * g:4 * g + 4, 128 * i:128 * (i + 1)],
                        bankspan(bk, 1).rearrange("p (a b) -> p a b", b=128))
                    s_p0e.inc(cp)

            if gamma_on:
                emit_gamma(4)

        if "xn1t" in dbg:
            s_d0 = sem("d0")

            @block.scalar
            def _(scalar):
                scalar.wait_ge(s_p0e.h, s_p0e.v)
                s_d0.inc(scalar.dma_start(out=dbg_ap("xn1t", [128, 16 * S], BF16),
                                          in_=r_xn1t.b16), 16)

        if live("p1"):
            r_x0.free()
            r_stg.free()
            r_qf = A.alloc_b16(H * S, "qf")
            r_kf = A.alloc_b16(KVH * S, "kf")
            r_vf = A.alloc_b16(KVH * S, "vf")
            r_wsl = A.alloc(D, "wslab")
            r_wb = [A.alloc_b16(D, "wb0"), A.alloc_b16(D, "wb1")]
            r_pst = A.alloc_b16(D, "p1stage")
            r_sh = A.alloc_b16(D, "p1shift")
            r_ra = A.alloc_b16(D, "p1a")
            r_rb = A.alloc_b16(D, "p1b")

            s_wld = sem("wld")
            s_wc = sem("wc")
            s_p1t = sem("p1t")
            s_p1c = sem("p1c")
            s_sh = sem("shd")
            s_rva = sem("rva")
            s_rse = sem("rse")
            s_rvq = sem("rvq")

            ROPE_OCS = 20

            for oc in range(NO):
                ri = min(oc, ROPE_OCS)

                @block.sync
                def _(sync, oc=oc):
                    if oc == 0:
                        sync.wait_ge(s_p0e.h, s_p0e.v)
                        sync.wait_ge(s_p0v.h, s_p0v.v)
                        sync.wait_ge(s_p0s.h, s_p0s.v)
                    else:
                        sync.wait_ge(s_wc.h, oc)
                    s_wld.inc(sync.dma_start(
                        out=r_wsl.f32.rearrange("p (a b) -> p a b", b=128),
                        in_=wqkv[oc].rearrange("a p b -> p a b")), 16)

                @block.scalar
                def _(scalar, oc=oc):
                    scalar.wait_ge(s_wld.h, 16 * (oc + 1))
                    if oc >= 2:
                        scalar.wait_ge(s_p1t.h, oc - 1)
                    s_wc.inc(scalar.copy(r_wb[oc % 2].b16, r_wsl.f32))

                @block.tensor
                def _(tensor, oc=oc):
                    tensor.wait_ge(s_wc.h, oc + 1)
                    if oc == 0:
                        tensor.wait_ge(s_p0e.h, s_p0e.v)
                    if oc >= 2:
                        tensor.wait_ge(s_p1c.h, oc - 1)
                    pb = 0 if oc % 2 == 0 else 4
                    for d in range(16):
                        for tb in range(4):
                            mm = tensor.matmul(
                                bank(pb + tb),
                                r_wb[oc % 2].b16[:, 128 * d:128 * (d + 1)],
                                r_xn1t.b16[:, 2048 * d + 512 * tb: 2048 * d + 512 * (tb + 1)],
                                start=(d == 0), stop=(d == 15))
                    s_p1t.inc(mm)

                @block.scalar
                def _(scalar, oc=oc, ri=ri):
                    scalar.wait_ge(s_p1t.h, oc + 1)
                    pb = 0 if oc % 2 == 0 else 4
                    if oc < ROPE_OCS:
                        if ri > 0:
                            scalar.wait_ge(s_rva.h, 2 * ri)
                            scalar.wait_ge(s_sh.h, 32 * ri)
                        cp = scalar.copy(r_pst.b16, bankspan(pb, 4))
                        s_p1c.inc(cp)
                        scalar.wait_ge(s_p1c.h, s_p1c.v)
                        if ri > 0:
                            scalar.wait_ge(s_rvq.h, ri)
                        s_sh.inc(scalar.dma_start(out=r_sh.b16[0:64, :],
                                                  in_=r_pst.b16[64:128, :]), 16)
                        s_sh.inc(scalar.dma_start(out=r_sh.b16[64:128, :],
                                                  in_=r_pst.b16[0:64, :]), 16)
                    else:
                        g = oc - 20
                        s_p1c.inc(scalar.copy(r_vf.b16[:, 2048 * g:2048 * (g + 1)],
                                              bankspan(pb, 4)))

                if oc < ROPE_OCS:
                    @block.vector
                    def _(vector, oc=oc, ri=ri):
                        vector.wait_ge(s_p1c.h, oc + 1)
                        if ri > 0:
                            vector.wait_ge(s_rvq.h, ri)
                        ia = vector.tensor_tensor(r_ra.b16, r_pst.b16, r_cos.b16, OP.mult)
                        s_rva.inc(ia)
                        vector.wait_ge(s_sh.h, 32 * (ri + 1))
                        ib = vector.tensor_tensor(r_rb.b16, r_sh.b16, r_sin.b16, OP.mult)
                        vbar(vector, ib)
                        ir = vector.tensor_tensor(r_ra.b16, r_ra.b16, r_rb.b16, OP.add)
                        vbar(vector, ir)
                        im = vector.tensor_scalar(r_rb.b16, r_ra.b16, 0.0, None, OP.min)
                        s_rva.inc(im)

                    @block.scalar
                    def _(scalar, ri=ri):
                        scalar.wait_ge(s_rva.h, 2 * ri + 2)
                        s_rse.inc(scalar.activation(r_rb.b16, r_rb.b16, AF.Exp))

                    @block.vector
                    def _(vector, oc=oc, ri=ri):
                        vector.wait_ge(s_rse.h, ri + 1)
                        if oc < 16:
                            dst = r_qf.b16[:, 2048 * oc:2048 * (oc + 1)]
                        else:
                            dst = r_kf.b16[:, 2048 * (oc - 16):2048 * (oc - 15)]
                        s_rvq.inc(vector.scalar_tensor_tensor(dst, r_ra.b16, 0.0, r_rb.b16,
                                                              OP.max, OP.add))
                if gamma_on:
                    emit_gamma(8)

            if "qf" in dbg:
                s_d1 = sem("d1")

                @block.scalar
                def _(scalar):
                    scalar.wait_ge(s_rvq.h, s_rvq.v)
                    scalar.wait_ge(s_p1c.h, s_p1c.v)
                    s_d1.inc(scalar.dma_start(out=dbg_ap("qf", [128, H * S], BF16),
                                              in_=r_qf.b16), 16)
                    s_d1.inc(scalar.dma_start(out=dbg_ap("kf", [128, KVH * S], BF16),
                                              in_=r_kf.b16), 16)
                    s_d1.inc(scalar.dma_start(out=dbg_ap("vf", [128, KVH * S], BF16),
                                              in_=r_vf.b16), 16)

            for r in (r_wsl, r_wb[0], r_wb[1], r_pst, r_sh, r_ra, r_rb, r_xn1t,
                      r_cos, r_sin):
                r.free()

        if live("p2a"):
            r_attb = A.alloc_b16(NCH * 2048, "attb")
            r_mask = A.alloc_b16(512, "mask4")
            r_ktk = A.alloc_b16(512, "ktok")
            r_vag = A.alloc_b16(4 * 132, "vaug")
            r_smt = A.alloc_b16(512, "smt")
            r_sta = A.alloc_b16(4 * 132, "stateaug")
            r_rec = A.alloc(8, "recden")

            s_mld = sem("mld")
            s2t = sem("s2t")
            s2v = sem("s2v")
            p1_c_done0 = s_p1c.v
            p1_q_done0 = s_rvq.v
            p1_sh_done0 = s_sh.v
            p1_wc_done0 = s_wc.v

            @block.sync
            def _(sync):
                sync.wait_ge(s_rvq.h, p1_q_done0)
                sync.wait_ge(s_p1c.h, p1_c_done0)
                sync.wait_ge(s_sh.h, p1_sh_done0)
                sync.wait_ge(s_wc.h, p1_wc_done0)
                s_mld.inc(sync.dma_start(out=r_mask.b16, in_=mask4), 16)

            @block.vector
            def _(vector):
                vector.wait_ge(s_sh.h, p1_sh_done0)
                m0 = vector.memset(r_vag.f32, 0.0)
                vbar(vector, m0)
                for g in range(4):
                    mi = vector.memset(r_vag.b16[:, 132 * g + 128:132 * g + 129], 1.0)
                s2v.inc(mi)
            v_vag_ones = s2v.v

            @block.tensor
            def _(tensor):
                tensor.wait_ge(s_fix.h, s_fix.v)
                tensor.matmul(bank(0, 500, 1), identb, identb[:, 0:1], start=True, stop=True)
                s2t.inc(tensor.matmul(bank(1, 500, 1), identb, identb[:, 0:1],
                                      start=True, stop=True))

            p1_c_done = s_p1c.v
            p1_q_done = s_rvq.v

            t_trans = {}
            v_tevac = {}
            t_scores = {}
            v_mask = {}
            t_att = {}
            v_atte = {}
            t_supd = [0]
            v_sevac = [0]

            for i in range(NCH):
                @block.tensor
                def _(tensor, i=i):
                    if i == 0:
                        tensor.wait_ge(s_rvq.h, p1_q_done)
                        tensor.wait_ge(s_p1c.h, p1_c_done)
                        tensor.wait_ge(s2v.h, v_vag_ones)
                    else:
                        tensor.wait_ge(s2v.h, v_mask[(i - 1, 3)])
                    for g in range(4):
                        mm = tensor.matmul(
                            bank(2, 128 * g, 128),
                            r_kf.b16[:, 2048 * g + 128 * i:2048 * g + 128 * (i + 1)],
                            identb, start=(g == 0), stop=(g == 3))
                    for g in range(4):
                        mm = tensor.matmul(
                            bank(3, 128 * g, 128),
                            r_vf.b16[:, 2048 * g + 128 * i:2048 * g + 128 * (i + 1)],
                            identb, start=(g == 0), stop=(g == 3))
                    t_trans[i] = s2t.inc(mm)

                @block.vector
                def _(vector, i=i):
                    vector.wait_ge(s2t.h, t_trans[i])
                    if i == 0:
                        vector.wait_ge(s_mld.h, 16)
                    vector.tensor_copy(r_ktk.b16, bankspan(2, 1))
                    cp = vector.tensor_copy(
                        r_vag.b16.rearrange("p (a b) -> p a b", b=132)[:, :, 0:128],
                        bankspan(3, 1).rearrange("p (a b) -> p a b", b=128))
                    s2v.inc(cp)
                    v_tevac[i] = s2v.v

                for g in range(4):
                    sb = 2 + (g % 2)

                    @block.tensor
                    def _(tensor, i=i, g=g, sb=sb):
                        tensor.wait_ge(s2v.h, v_tevac[i])
                        if g >= 2:
                            tensor.wait_ge(s2v.h, v_mask[(i, g - 2)])
                        for j in range(4):
                            h = 4 * g + j
                            mm = tensor.matmul(
                                bank(sb, 128 * j, 128),
                                r_kf.b16[:, 2048 * g + 128 * i:2048 * g + 128 * (i + 1)],
                                r_qf.b16[:, 2048 * h + 128 * i:2048 * h + 128 * (i + 1)],
                                start=(j == 0), stop=(j == 3))
                        t_scores[(i, g)] = s2t.inc(mm)

                    @block.vector
                    def _(vector, i=i, g=g, sb=sb):
                        vector.wait_ge(s2t.h, t_scores[(i, g)])
                        if g > 0:
                            vector.wait_ge(s2t.h, t_att[(i, 4 * g - 1)])
                        elif i > 0:
                            vector.wait_ge(s2t.h, t_att[(i - 1, 15)])
                        s2v.inc(vector.tensor_tensor(r_smt.b16, bankspan(sb, 1),
                                                     r_mask.b16, OP.mult))
                        v_mask[(i, g)] = s2v.v

                    for j in range(4):
                        h = 4 * g + j
                        ab = 4 + (h % 2)

                        @block.tensor
                        def _(tensor, i=i, g=g, j=j, h=h, ab=ab):
                            tensor.wait_ge(s2v.h, v_mask[(i, g)])
                            if h >= 2:
                                tensor.wait_ge(s2v.h, v_atte[(i, h - 2)])
                            elif i > 0:
                                tensor.wait_ge(s2v.h, v_atte[(i - 1, 14 + h)])
                            mm = tensor.matmul(bank(ab, 0, 129),
                                               r_smt.b16[:, 128 * j:128 * (j + 1)],
                                               r_vag.b16[:, 132 * g:132 * g + 129],
                                               start=True, stop=(i == 0))
                            if i > 0:
                                mm = tensor.matmul(
                                    bank(ab, 0, 129),
                                    r_qf.b16[:, 2048 * h + 128 * i:2048 * h + 128 * (i + 1)],
                                    r_sta.b16[:, 132 * g:132 * g + 129],
                                    start=False, stop=True)
                            t_att[(i, h)] = s2t.inc(mm)

                        @block.vector
                        def _(vector, i=i, h=h, ab=ab):
                            vector.wait_ge(s2t.h, t_att[(i, h)])
                            cc = r_rec.f32[:, (h % 2):(h % 2) + 1]
                            i1 = vector.tensor_scalar(cc, bank(ab, 128, 1), EPS_ATT,
                                                      None, OP.add)
                            vbar(vector, i1)
                            i2 = vector.reciprocal(cc, cc)
                            vbar(vector, i2)
                            i3 = vector.tensor_scalar(
                                r_attb.b16[:, 2048 * i + 128 * h:2048 * i + 128 * (h + 1)],
                                bank(ab, 0, 128), cc, None, OP.mult)
                            s2v.inc(i3)
                            v_atte[(i, h)] = s2v.v

                if i < NCH - 1:
                    @block.tensor
                    def _(tensor, i=i):
                        if i > 0:
                            tensor.wait_ge(s2v.h, v_sevac[0])
                        for g in range(4):
                            mm = tensor.matmul(bank(g // 2, 140 * (g % 2), 129),
                                               r_ktk.b16[:, 128 * g:128 * (g + 1)],
                                               r_vag.b16[:, 132 * g:132 * g + 129],
                                               start=False, stop=False,
                                               skip_group_check=True)
                        t_supd[0] = s2t.inc(mm)

                    @block.vector
                    def _(vector, i=i):
                        vector.wait_ge(s2t.h, t_supd[0])
                        for g in range(4):
                            cp = vector.tensor_copy(r_sta.b16[:, 132 * g:132 * g + 129],
                                                    bank(g // 2, 140 * (g % 2), 129))
                        v_sevac[0] = s2v.inc(cp)

                if gamma_on:
                    emit_gamma(11)

            if "attb" in dbg:
                s_d2 = sem("d2")

                @block.scalar
                def _(scalar):
                    scalar.wait_ge(s2v.h, s2v.v)
                    s_d2.inc(scalar.dma_start(
                        out=dbg_ap("attb", [S, H * HD], BF16).rearrange(
                            "(i p) o -> p i o", p=128),
                        in_=r_attb.b16.rearrange("p (i o) -> p i o", o=2048)), 16)

            for r in (r_ktk, r_vag, r_smt, r_sta, r_rec, r_mask, r_qf, r_kf, r_vf):
                r.free()
            p2a_v_final = s2v.v

        if live("p2b"):
            r_att_t = [A.alloc_b16(2048, "attT0"), A.alloc_b16(2048, "attT1")]
            r_wotb = A.alloc_b16(16 * 2048, "wotb")
            r_wosl = A.alloc(D, "woslab")
            r_x0b = A.alloc(D, "x0re")
            r_x1b = A.alloc(D, "x1buf")

            s_wold = sem("wold")
            s_woc = sem("woc")
            s_x0b = sem("x0re")
            s_x1w = sem("x1w")
            s2bt = sem("s2bt")
            s2bv = sem("s2bv")
            s2bs = sem("s2bs")

            @block.sync
            def _(sync):
                sync.wait_ge(s2v.h, p2a_v_final)
                s_x0b.inc(sync.dma_start(out=r_x0b.f32, in_=x0[0:128, :]), 16)

            for oc in range(16):
                @block.sync
                def _(sync, oc=oc):
                    if oc > 0:
                        sync.wait_ge(s_woc.h, oc)
                    s_wold.inc(sync.dma_start(out=r_wosl.f32,
                                              in_=wo[128 * oc:128 * (oc + 1), :]), 16)

                @block.scalar
                def _(scalar, oc=oc):
                    scalar.wait_ge(s_wold.h, 16 * (oc + 1))
                    s_woc.inc(scalar.copy(r_wotb.b16[:, 2048 * oc:2048 * (oc + 1)],
                                          r_wosl.f32))

            v_ate = {}
            t_wo = {}
            v_x1ps = {}
            v_resid = {}
            v_xn2 = {}

            for i in range(NCH):
                for g in range(4):
                    ab = 4 + (g % 2)

                    @block.tensor
                    def _(tensor, i=i, g=g, ab=ab):
                        if i == 0 and g == 0:
                            tensor.wait_ge(s2v.h, p2a_v_final)
                        gid = 4 * i + g
                        if gid >= 2:
                            prev = (i, g - 2) if g >= 2 else (i - 1, g + 2)
                            tensor.wait_ge(s2bv.h, v_ate[prev])
                        for d4 in range(4):
                            oc = 4 * g + d4
                            mm = tensor.matmul(
                                bank(ab, 128 * d4, 128),
                                r_attb.b16[:, 2048 * i + 128 * oc:2048 * i + 128 * (oc + 1)],
                                identb, start=(d4 == 0), stop=(d4 == 3))
                        s2bt.inc(mm)

                    @block.vector
                    def _(vector, i=i, g=g, ab=ab):
                        vector.wait_ge(s2bt.h, s2bt.v)
                        att3 = r_att_t[i % 2].b16.rearrange("p (a b) -> p a b", b=128)
                        cp = vector.tensor_copy(
                            att3[:, 4 * g:4 * g + 4, :],
                            bankspan(ab, 1).rearrange("p (a b) -> p a b", b=128))
                        s2bv.inc(cp)
                        v_ate[(i, g)] = s2bv.v

                @block.tensor
                def _(tensor, i=i):
                    tensor.wait_ge(s2bv.h, v_ate[(i, 3)])
                    if i > 0:
                        tensor.wait_ge(s2bv.h, v_x1ps[i - 1])
                        tensor.wait_ge(s2bs.h, s2bs.v)
                    for oc in range(16):
                        if i == 0:
                            tensor.wait_ge(s_woc.h, oc + 1)
                        for b in range(4):
                            mm = tensor.matmul(
                                bank(b), r_att_t[i % 2].b16[:, 128 * oc:128 * (oc + 1)],
                                r_wotb.b16[:, 2048 * oc + 512 * b:2048 * oc + 512 * (b + 1)],
                                start=(oc == 0), stop=(oc == 15))
                    t_wo[i] = s2bt.inc(mm)

                @block.vector
                def _(vector, i=i):
                    vector.wait_ge(s2bt.h, t_wo[i])
                    i1 = vector.reduce_sum(r_cols.f32[:, 0:1], PSA[:, :], axis=AXL.X)
                    s2bv.inc(i1)

                v_worad = s2bv.v

                @block.scalar
                def _(scalar, i=i, v_worad=v_worad):
                    scalar.wait_ge(s2bv.h, v_worad)
                    if i > 0:
                        scalar.wait_ge(s_x1w.h, 16 * i)
                    scalar.activation(trashA, PSA[:, 0:1024], AF.Square)
                    i2 = scalar.activation(r_x1b.b16[:, 0:1024], PSA[:, 1024:2048],
                                           AF.Square)
                    s2bs.inc(i2)

                @block.vector
                def _(vector, i=i):
                    vector.wait_ge(s2bs.h, s2bs.v)
                    vector.reduce_sum(r_cols.f32[:, 6:7], trashA, axis=AXL.X)
                    i3 = vector.reduce_sum(r_cols.f32[:, 1:2], r_x1b.b16[:, 0:1024],
                                           axis=AXL.X)
                    vbar(vector, i3)
                    i4 = vector.tensor_tensor(r_cols.f32[:, 1:2], r_cols.f32[:, 1:2],
                                              r_cols.f32[:, 6:7], OP.add)
                    s2bv.inc(i4)

                ln_cols(s2bv, s2bs, D)

                @block.vector
                def _(vector, i=i):
                    cc = r_cols.f32
                    if i > 0:
                        vector.wait_ge(s_x1w.h, 16 * i)
                        vector.wait_ge(s2bv.h, v_xn2[i - 1])
                    i1 = vector.tensor_scalar(r_x1b.f32, PSA[:, :], cc[:, 4:5], cc[:, 5:6],
                                              OP.mult, OP.add)
                    v_x1ps[i] = s2bv.inc(i1)
                    vector.wait_ge(s2bv.h, s2bv.v)
                    vector.wait_ge(s_x0b.h, 16 * (i + 1))
                    i2 = vector.tensor_tensor(r_x1b.f32, r_x1b.f32, r_x0b.f32, OP.add)
                    v_resid[i] = s2bv.inc(i2)
                    vector.wait_ge(s2bv.h, s2bv.v)
                    vector.reduce_sum(cc[:, 0:1], r_x1b.f32, axis=AXL.X)

                @block.scalar
                def _(scalar, i=i):
                    scalar.wait_ge(s2bv.h, v_resid[i])
                    s_x1w.inc(scalar.dma_start(out=x1s[128 * i:128 * (i + 1), :],
                                               in_=r_x1b.f32), 16)
                    scalar.activation(trashA, r_x1b.f32[:, 0:1024], AF.Square)
                    i2 = scalar.activation(r_attb.b16[:, 2048 * i:2048 * i + 1024],
                                           r_x1b.f32[:, 1024:2048], AF.Square)
                    s2bs.inc(i2)

                @block.vector
                def _(vector, i=i):
                    vector.wait_ge(s2bs.h, s2bs.v)
                    vector.reduce_sum(r_cols.f32[:, 6:7], trashA, axis=AXL.X)
                    i3 = vector.reduce_sum(r_cols.f32[:, 1:2],
                                           r_attb.b16[:, 2048 * i:2048 * i + 1024],
                                           axis=AXL.X)
                    vbar(vector, i3)
                    i4 = vector.tensor_tensor(r_cols.f32[:, 1:2], r_cols.f32[:, 1:2],
                                              r_cols.f32[:, 6:7], OP.add)
                    s2bv.inc(i4)

                ln_cols(s2bv, s2bs, D, rscale=0.5)

                @block.vector
                def _(vector, i=i):
                    cc = r_cols.f32
                    i1 = vector.tensor_scalar(r_attb.b16[:, 2048 * i:2048 * (i + 1)],
                                              r_x1b.f32, cc[:, 4:5], cc[:, 5:6],
                                              OP.mult, OP.add)
                    v_xn2[i] = s2bv.inc(i1)

                if i + 1 < NCH:
                    @block.sync
                    def _(sync, i=i):
                        sync.wait_ge(s2bv.h, v_resid[i])
                        s_x0b.inc(sync.dma_start(out=r_x0b.f32,
                                                 in_=x0[128 * (i + 1):128 * (i + 2), :]), 16)

                if gamma_on:
                    emit_gamma(6)

            if "x1" in dbg:
                s_d4 = sem("d4")

                @block.scalar
                def _(scalar):
                    scalar.wait_ge(s_x1w.h, s_x1w.v)
                    s_d4.inc(scalar.dma_start(out=dbg_ap("x1", [S, D]), in_=x1s), 16)

        if live("gamma"):
            emit_gamma(NGTOT)
            r_gtmp = A.alloc(256, "gtmp")
            s_gf = sem("gfin")

            @block.scalar
            def _(scalar):
                scalar.wait_ge(s_gvs.h, s_gvs.v)
                for w in range(3):
                    s_gf.inc(scalar.dma_start(out=gsc[w], in_=r_gsum.f32[:, w:w + 1]), 16)

            for w in range(3):
                @block.scalar
                def _(scalar, w=w):
                    scalar.wait_ge(s_gf.h, s_gf.v)
                    s_gf.inc(scalar.dma_start(out=r_gtmp.f32[0:1, 0:128],
                                              in_=gsc[w].rearrange("(a b) -> a b", a=1)), 16)

                @block.vector
                def _(vector, w=w):
                    vector.wait_ge(s_gf.h, s_gf.v)
                    cc = r_cols.f32
                    i1 = vector.reduce_sum(cc[0:1, 7:8], r_gtmp.f32[0:1, 0:128], axis=AXL.X)
                    vbar(vector, i1)
                    i2 = vector.tensor_scalar(cc[0:1, 7:8], cc[0:1, 7:8], 0.5 / NW,
                                              0.5 * EPS_Q, OP.mult, OP.add)
                    vbar(vector, i2)
                    i3 = vector.tensor_scalar(r_gtmp.f32[0:1, 0:128], r_gtmp.f32[0:1, 0:128],
                                              0.0, cc[0:1, 7:8], OP.mult, OP.add)
                    vbar(vector, i3)
                    i4 = vector.tensor_scalar(r_gtmp.f32[0:1, 128:256],
                                              r_gtmp.f32[0:1, 0:128], -1.0, None, OP.mult)
                    s_gf.inc(i4)

                @block.scalar
                def _(scalar, w=w):
                    scalar.wait_ge(s_gf.h, s_gf.v)
                    s_gf.inc(scalar.dma_start(out=gsc[4 + 2 * w],
                                              in_=r_gtmp.f32[0:1, 0:128]), 16)
                    s_gf.inc(scalar.dma_start(out=gsc[5 + 2 * w],
                                              in_=r_gtmp.f32[0:1, 128:256]), 16)
                    scalar.wait_ge(s_gf.h, s_gf.v)
                    s_gf.inc(scalar.dma_start(
                        out=r_tq.f32[:, 2 * w:2 * w + 1],
                        in_=gsc[4 + 2 * w].rearrange("(a b) -> a b", b=1)), 16)
                    s_gf.inc(scalar.dma_start(
                        out=r_tq.f32[:, 2 * w + 1:2 * w + 2],
                        in_=gsc[5 + 2 * w].rearrange("(a b) -> a b", b=1)), 16)
            gamma_ready = s_gf.v
            r_gtmp.free()
            r_gs[0].free()
            r_gs[1].free()

            if "tq" in dbg:
                s_d3 = sem("d3")

                @block.scalar
                def _(scalar):
                    scalar.wait_ge(s_gf.h, gamma_ready)
                    s_d3.inc(scalar.dma_start(out=dbg_ap("tq", [128, 8]), in_=r_tq.f32), 16)

        if live("xn2t"):
            r_att_t[0].free()
            r_att_t[1].free()
            r_wotb.free()
            r_wosl.free()
            r_x0b.free()
            r_x1b.free()
            r_xn2t = A.alloc_b16(16 * S, "xn2t")
            xn2t3 = r_xn2t.b16.rearrange("p (dc t) -> p dc t", t=S)
            s_xtt = sem("xtt")
            s_xte = sem("xte")

            for i in range(NCH):
                for g in range(4):
                    bk = 4 + (g % 2)

                    @block.tensor
                    def _(tensor, i=i, g=g, bk=bk):
                        if i == 0 and g == 0:
                            tensor.wait_ge(s2bv.h, v_xn2[NCH - 1])
                        gidx = i * 4 + g
                        if gidx >= 2:
                            tensor.wait_ge(s_xte.h, gidx - 1)
                        for d4 in range(4):
                            d = 4 * g + d4
                            mm = tensor.matmul(
                                bank(bk, 128 * d4, 128),
                                r_attb.b16[:, 2048 * i + 128 * d:2048 * i + 128 * (d + 1)],
                                identb, start=(d4 == 0), stop=(d4 == 3))
                        s_xtt.inc(mm)

                    @block.vector
                    def _(vector, i=i, g=g, bk=bk):
                        vector.wait_ge(s_xtt.h, s_xtt.v)
                        cp = vector.tensor_copy(
                            xn2t3[:, 4 * g:4 * g + 4, 128 * i:128 * (i + 1)],
                            bankspan(bk, 1).rearrange("p (a b) -> p a b", b=128))
                        s_xte.inc(cp)

            if "xn2t" in dbg:
                s_d5 = sem("d5")

                @block.scalar
                def _(scalar):
                    scalar.wait_ge(s_xte.h, s_xte.v)
                    s_d5.inc(scalar.dma_start(out=dbg_ap("xn2t", [128, 16 * S], BF16),
                                              in_=r_xn2t.b16), 16)

            r_attb.free()

        if live("p3"):
            r_q = [A.alloc_b16(2048, "q0"), A.alloc_b16(2048, "q1")]
            r_wstr = [A.alloc(2048, "wstrip0"), A.alloc(2048, "wstrip1")]
            r_sgb = A.alloc_b16(2048, "signb")
            r_w3q = [A.alloc_b16(1024, f"w3q{h}") for h in range(NHC)]
            r_w3s = [A.alloc(1024, "w3strip0"), A.alloc(1024, "w3strip1")]
            r_w3bb = A.alloc_b16(1024, "w3signb")
            r_hid = [A.alloc_b16(2048, "hid0"), A.alloc_b16(2048, "hid1")]
            r_sig = A.alloc(1024, "sig")

            s3ld = [sem("s3ld0"), sem("s3ld1")]
            s3qs = sem("s3qs")
            s3qv = sem("s3qv")
            s3t = sem("s3t")
            s3s = sem("s3s")
            s3v = sem("s3v")
            s3h = [sem("s3h0"), sem("s3h1")]
            s3w3d = [sem("s3w3d0"), sem("s3w3d1")]
            s3w3s = sem("s3w3s")
            s3w3v = sem("s3w3v")

            def strip_done_tval(s):
                hc, k = s // 2, s % 2
                return 4 * hc + 2 * (k + 1)

            xte_final = s_xte.v
            x1w_final = s_x1w.v

            for s in range(2 * NHC):
                hc, k = s // 2, s % 2
                wsrc = w1b if k == 0 else w2b

                @block.sync
                def _(sync, s=s, hc=hc, wsrc=wsrc):
                    if s <= 1:
                        sync.wait_ge(s_xte.h, xte_final)
                        sync.wait_ge(s_x1w.h, x1w_final)
                    else:
                        sync.wait_ge(s3qs.h, s - 1)
                    s3ld[s % 2].inc(sync.dma_start(
                        out=r_wstr[s % 2].f32.rearrange("p (a b) -> p a b", b=128),
                        in_=wsrc[hc].rearrange("(a p) b -> p a b", p=128)), 16)

                @block.scalar
                def _(scalar, s=s, k=k):
                    scalar.wait_ge(s3ld[s % 2].h, 16 * (s // 2 + 1))
                    if s == 0:
                        scalar.wait_ge(s_gf.h, gamma_ready)
                    if s >= 2:
                        scalar.wait_ge(s3t.h, strip_done_tval(s - 2))
                    if s >= 1:
                        scalar.wait_ge(s3qv.h, s)
                    w = k
                    scalar.activation(r_q[s % 2].b16, r_wstr[s % 2].f32, AF.Sign,
                                      bias=r_tq.f32[:, 2 * w + 1:2 * w + 2])
                    i2 = scalar.activation(r_sgb.b16, r_wstr[s % 2].f32, AF.Sign,
                                           bias=r_tq.f32[:, 2 * w:2 * w + 1])
                    s3qs.inc(i2)

                @block.vector
                def _(vector, s=s):
                    vector.wait_ge(s3qs.h, s + 1)
                    s3qv.inc(vector.tensor_tensor(r_q[s % 2].b16, r_q[s % 2].b16,
                                                  r_sgb.b16, OP.add))

                for th in range(2):
                    x1b0 = 0 if th == 0 else 4
                    x2b0 = 2 if th == 0 else 6
                    pb = x1b0 if k == 0 else x2b0

                    @block.tensor
                    def _(tensor, s=s, hc=hc, k=k, th=th, pb=pb):
                        tensor.wait_ge(s3qv.h, s + 1)
                        if s == 0 and th == 0:
                            tensor.wait_ge(s_xte.h, s_xte.v)
                        if k == 0 and hc > 0:
                            tensor.wait_ge(s3s.h, 2 * (hc - 1) + th + 1)
                        if k == 1 and hc > 0:
                            tensor.wait_ge(s3v.h, 2 * (hc - 1) + th + 1)
                        for d in range(16):
                            for b2 in range(2):
                                mm = tensor.matmul(
                                    bank(pb + b2),
                                    r_q[s % 2].b16[:, 128 * d:128 * (d + 1)],
                                    r_xn2t.b16[:, 2048 * d + 1024 * th + 512 * b2:
                                               2048 * d + 1024 * th + 512 * (b2 + 1)],
                                    start=(d == 0), stop=(d == 15))
                        s3t.inc(mm)

                if k == 1:
                    for th in range(2):
                        x1b0 = 0 if th == 0 else 4
                        x2b0 = 2 if th == 0 else 6
                        x1_done = 4 * hc + 1 + th
                        x2_done = 4 * hc + 3 + th

                        @block.scalar
                        def _(scalar, hc=hc, th=th, x1b0=x1b0, x1_done=x1_done):
                            scalar.wait_ge(s3t.h, x1_done)
                            if hc > 0 or th > 0:
                                scalar.wait_ge(s3v.h, 2 * hc + th)
                            s3s.inc(scalar.activation(r_sig.f32, bankspan(x1b0, 2),
                                                      AF.Tanh, scale=0.5))

                        @block.vector
                        def _(vector, hc=hc, th=th, x2b0=x2b0, x2_done=x2_done):
                            vector.wait_ge(s3t.h, x2_done)
                            vector.wait_ge(s3s.h, 2 * hc + th + 1)
                            if hc >= 2:
                                vector.wait_ge(s3h[hc % 2].h, 16 * (hc // 2))
                            i0 = vector.tensor_scalar(r_sig.f32, r_sig.f32, 0.25, 0.25,
                                                      OP.mult, OP.add)
                            vbar(vector, i0)
                            s3v.inc(vector.scalar_tensor_tensor(
                                r_hid[hc % 2].b16[:, 1024 * th:1024 * (th + 1)],
                                bankspan(x2b0, 2), 1.0, r_sig.f32, OP.mult, OP.mult))

                    @block.scalar
                    def _(scalar, hc=hc):
                        scalar.wait_ge(s3v.h, 2 * hc + 2)
                        s3h[hc % 2].inc(scalar.dma_start(out=hids[hc],
                                                         in_=r_hid[hc % 2].b16), 16)

                    @block.sync
                    def _(sync, hc=hc):
                        if hc <= 1:
                            sync.wait_ge(s_xte.h, xte_final)
                            sync.wait_ge(s_x1w.h, x1w_final)
                        else:
                            sync.wait_ge(s3w3s.h, hc - 1)
                        s3w3d[hc % 2].inc(sync.dma_start(out=r_w3s[hc % 2].f32,
                                                         in_=w3b[0, hc]), 16)

                    @block.scalar
                    def _(scalar, hc=hc):
                        scalar.wait_ge(s3w3d[hc % 2].h, 16 * (hc // 2 + 1))
                        if hc >= 1:
                            scalar.wait_ge(s3w3v.h, hc)
                        scalar.activation(r_w3q[hc].b16, r_w3s[hc % 2].f32, AF.Sign,
                                          bias=r_tq.f32[:, 5:6])
                        s3w3s.inc(scalar.activation(r_w3bb.b16, r_w3s[hc % 2].f32,
                                                    AF.Sign, bias=r_tq.f32[:, 4:5]))

                    @block.vector
                    def _(vector, hc=hc):
                        vector.wait_ge(s3w3s.h, hc + 1)
                        s3w3v.inc(vector.tensor_tensor(r_w3q[hc].b16, r_w3q[hc].b16,
                                                       r_w3bb.b16, OP.add))

            if "q1s" in dbg:
                s_d6 = sem("d6")

                @block.scalar
                def _(scalar):
                    scalar.wait_ge(s3qv.h, 1)
                    s_d6.inc(scalar.dma_start(out=dbg_ap("q1s", [128, 2048], BF16),
                                              in_=r_q[0].b16), 16)

            if "hid" in dbg:
                s_d7 = sem("d7")

                @block.scalar
                def _(scalar):
                    scalar.wait_ge(s3h[0].h, s3h[0].v)
                    scalar.wait_ge(s3h[1].h, s3h[1].v)
                    s_d7.inc(scalar.dma_start(out=dbg_ap("hid", [NHC, 128, S], BF16),
                                              in_=hids), 16)

            for r in (r_q[0], r_q[1], r_wstr[0], r_wstr[1], r_sgb, r_hid[0], r_hid[1],
                      r_sig, r_xn2t):
                r.free()
            p3_h_final = (s3h[0].v, s3h[1].v)
            p3_t_final = s3t.v

        if live("p4"):
            r_stg4 = [A.alloc(1024, "p4stg0"), A.alloc(1024, "p4stg1")]
            r_p4st = A.alloc(32, "p4stats")
            r_hsl = [A.alloc_b16(11 * 128, f"hsl{j}") for j in range(3)]
            r_w3q2 = [A.alloc_b16(1024, f"w3q2_{h}") for h in range(NHC)]

            s4ld = [sem(f"s4ld{j}") for j in range(3)]
            s4st = [sem("s4st0"), sem("s4st1")]
            s4x1 = sem("s4x1")
            s4t = sem("s4t")
            s4tq = sem("s4tq")
            s4v = sem("s4v")
            s4s = sem("s4s")
            s4w = [sem("s4w0"), sem("s4w1")]
            s4y = sem("s4y")
            s4w3d = [sem("s4w3d0"), sem("s4w3d1")]
            s4w3s = sem("s4w3s")
            s4w3v = sem("s4w3v")

            v4ps = {}

            def emit_slice_dmas(qgs, first_wait_p3=False):
                @block.sync
                def _(sync):
                    for qg in qgs:
                        rnd, rem = divmod(qg, 64)
                        t1, q = divmod(rem, 4)
                        if first_wait_p3 and qg == 0:
                            sync.wait_ge(s3h[0].h, p3_h_final[0])
                            sync.wait_ge(s3h[1].h, p3_h_final[1])
                        if qg >= 3:
                            sync.wait_ge(s4tq.h, qg - 2)
                        src = hids[11 * q:11 * (q + 1), :, 128 * t1:128 * (t1 + 1)]
                        s4ld[qg % 3].inc(sync.dma_start(
                            out=r_hsl[qg % 3].b16.rearrange("p (a b) -> p a b", b=128),
                            in_=src.rearrange("a p b -> p a b")), 16)

            emit_slice_dmas(range(0, 4), first_wait_p3=True)

            if "hsl0" in dbg:
                s_da = sem("da")

                @block.scalar
                def _(scalar):
                    scalar.wait_ge(s4ld[0].h, 16)
                    s_da.inc(scalar.dma_start(out=dbg_ap("hsl0", [128, 1408], BF16),
                                              in_=r_hsl[0].b16), 16)

            for rnd in range(p4_rounds):
                if rnd == 1:
                    r_w3s[0].free()
                    r_w3s[1].free()
                    r_w3bb.free()
                    for _r in r_w3q:
                        _r.free()
                    r_stg4[0].free()
                    r_stg4[1].free()
                    r_sst = [A.alloc(1024, "rst0"), A.alloc(1024, "rst1")]
                    r_x1rd = A.alloc(2048, "x1rd")
                    r_yst = A.alloc(2048, "ystage")

                w3q_cur = r_w3q if rnd == 0 else r_w3q2

                for t1 in range(NCH):
                    ti = rnd * NCH + t1
                    pb = 0 if t1 % 2 == 0 else 2

                    if rnd == 1:
                        @block.sync
                        def _(sync, t1=t1):
                            if t1 == 0:
                                sync.wait_ge(s4tq.h, 4 * NCH)
                                sync.wait_ge(s4w[0].h, s4w[0].v)
                                sync.wait_ge(s4w[1].h, s4w[1].v)
                            if t1 >= 2:
                                sync.wait_ge(s4v.h, v4ps[NCH + t1 - 2])
                            s4st[t1 % 2].inc(sync.dma_start(out=r_sst[t1 % 2].f32,
                                                            in_=stash[t1]), 16)
                            if t1 >= 1:
                                sync.wait_ge(s4v.h, v4ps[NCH + t1 - 1] + 1)
                            s4x1.inc(sync.dma_start(out=r_x1rd.f32,
                                                    in_=x1s[128 * t1:128 * (t1 + 1), :]), 16)

                    @block.tensor
                    def _(tensor, rnd=rnd, t1=t1, ti=ti, pb=pb, w3q_cur=w3q_cur):
                        if ti >= 2:
                            tensor.wait_ge(s4v.h, v4ps[ti - 2])
                        if rnd == 0 and t1 == 0:
                            tensor.wait_ge(s3w3v.h, NHC)
                        if rnd == 1 and t1 == 0:
                            tensor.wait_ge(s4w3v.h, NHC)
                        for q in range(4):
                            qg = 64 * rnd + 4 * t1 + q
                            tensor.wait_ge(s4ld[qg % 3].h, 16 * (qg // 3 + 1))
                            buf = r_hsl[qg % 3]
                            if not (p4_parts & 4):
                                s4tq.inc(tensor.nop())
                                continue
                            for hh in range(11):
                                hc = 11 * q + hh
                                for b in range(2):
                                    mm = tensor.matmul(
                                        bank(pb + b),
                                        buf.b16[:, 128 * hh:128 * (hh + 1)],
                                        w3q_cur[hc].b16[:, 512 * b:512 * (b + 1)],
                                        start=(hc == 0), stop=(hc == NHC - 1))
                            s4tq.inc(mm)

                    nxt = 64 * rnd + 4 * (t1 + 1)
                    if rnd == 0 or t1 + 1 < NCH:
                        emit_slice_dmas(range(nxt, min(nxt + 4, 128)))

                    if rnd == 0 and not (p4_parts & 1):
                        @block.vector
                        def _(vector, t1=t1, ti=ti, pb=pb):
                            vector.wait_ge(s4tq.h, 4 * (ti + 1))
                            v4ps[ti] = s4v.inc(vector.memset(r_p4st.f32[:, 2 * t1:2 * t1 + 1], 0.0))
                    if rnd == 0 and (p4_parts & 1):
                        @block.vector
                        def _(vector, t1=t1, ti=ti, pb=pb):
                            vector.wait_ge(s4tq.h, 4 * (ti + 1))
                            if t1 >= 2:
                                vector.wait_ge(s4w[t1 % 2].h, 16 * (t1 // 2))
                            vector.tensor_copy(r_stg4[t1 % 2].f32, bankspan(pb, 2))
                            i2 = vector.reduce_sum(r_p4st.f32[:, 2 * t1:2 * t1 + 1],
                                                   bankspan(pb, 2), axis=AXL.X)
                            s4v.inc(i2)

                        sv_vrd = s4v.v

                        @block.scalar
                        def _(scalar, t1=t1, ti=ti, pb=pb, sv_vrd=sv_vrd):
                            scalar.wait_ge(s4v.h, sv_vrd)
                            s4s.inc(scalar.activation(trashA, bankspan(pb, 2), AF.Square))

                        @block.vector
                        def _(vector, t1=t1, ti=ti, pb=pb):
                            vector.wait_ge(s4s.h, s4s.v)
                            i3 = vector.reduce_sum(r_p4st.f32[:, 2 * t1 + 1:2 * t1 + 2],
                                                   trashA, axis=AXL.X)
                            v4ps[ti] = s4v.inc(i3)

                        @block.scalar
                        def _(scalar, t1=t1, ti=ti):
                            scalar.wait_ge(s4v.h, v4ps[ti])
                            s4w[t1 % 2].inc(scalar.dma_start(out=stash[t1],
                                                             in_=r_stg4[t1 % 2].f32), 16)


                        if t1 == 0 and "w3q0" in dbg:
                            s_d9 = sem("d9")

                            @block.scalar
                            def _(scalar):
                                scalar.wait_ge(s3w3v.h, NHC)
                                s_d9.inc(scalar.dma_start(
                                    out=dbg_ap("w3q0", [128, 1024], BF16),
                                    in_=r_w3q[0].b16), 16)
                                s_d9.inc(scalar.dma_start(
                                    out=dbg_ap("w3q7", [128, 1024], BF16),
                                    in_=r_w3q[7].b16), 16)

                        if t1 == NCH - 1 and "sth" in dbg:
                            s_d8 = sem("d8")

                            @block.scalar
                            def _(scalar):
                                scalar.wait_ge(s4w[0].h, s4w[0].v)
                                scalar.wait_ge(s4w[1].h, s4w[1].v)
                                s_d8.inc(scalar.dma_start(
                                    out=dbg_ap("sth", [NCH, 128, 1024]), in_=stash), 16)

                        for hc3 in (range(3 * t1, min(3 * (t1 + 1), NHC)) if (p4_parts & 2) else []):
                            @block.sync
                            def _(sync, hc3=hc3):
                                if hc3 <= 1:
                                    sync.wait_ge(s3w3s.h, NHC)
                                else:
                                    sync.wait_ge(s4w3s.h, hc3 - 1)
                                s4w3d[hc3 % 2].inc(sync.dma_start(
                                    out=r_w3s[hc3 % 2].f32, in_=w3b[1, hc3]), 16)

                            @block.scalar
                            def _(scalar, hc3=hc3):
                                scalar.wait_ge(s4w3d[hc3 % 2].h, 16 * (hc3 // 2 + 1))
                                if hc3 == 0:
                                    scalar.wait_ge(s3w3v.h, NHC)
                                    scalar.wait_ge(s3t.h, p3_t_final)
                                else:
                                    scalar.wait_ge(s4w3v.h, hc3)
                                scalar.activation(r_w3q2[hc3].b16, r_w3s[hc3 % 2].f32,
                                                  AF.Sign, bias=r_tq.f32[:, 5:6])
                                s4w3s.inc(scalar.activation(r_w3bb.b16,
                                                            r_w3s[hc3 % 2].f32,
                                                            AF.Sign,
                                                            bias=r_tq.f32[:, 4:5]))

                            @block.vector
                            def _(vector, hc3=hc3):
                                vector.wait_ge(s4w3s.h, hc3 + 1)
                                s4w3v.inc(vector.tensor_tensor(r_w3q2[hc3].b16,
                                                               r_w3q2[hc3].b16,
                                                               r_w3bb.b16, OP.add))
                    if rnd == 1:
                        @block.vector
                        def _(vector, t1=t1, ti=ti, pb=pb):
                            vector.wait_ge(s4tq.h, 4 * (ti + 1))
                            cc = r_cols.f32
                            i1 = vector.reduce_sum(cc[:, 0:1], bankspan(pb, 2), axis=AXL.X)
                            s4v.inc(i1)
                            vector.wait_ge(s4v.h, s4v.v)
                            vector.tensor_tensor(cc[:, 0:1], cc[:, 0:1],
                                                 r_p4st.f32[:, 2 * t1:2 * t1 + 1], OP.add)

                        sv_vrd1 = s4v.v

                        @block.scalar
                        def _(scalar, t1=t1, ti=ti, pb=pb, sv_vrd1=sv_vrd1):
                            scalar.wait_ge(s4v.h, sv_vrd1)
                            s4s.inc(scalar.activation(trashA, bankspan(pb, 2), AF.Square))

                        @block.vector
                        def _(vector, t1=t1, ti=ti, pb=pb):
                            cc = r_cols.f32
                            vector.wait_ge(s4s.h, s4s.v)
                            i3 = vector.reduce_sum(cc[:, 1:2], trashA, axis=AXL.X)
                            vbar(vector, i3)
                            i4 = vector.tensor_tensor(cc[:, 1:2], cc[:, 1:2],
                                                      r_p4st.f32[:, 2 * t1 + 1:2 * t1 + 2],
                                                      OP.add)
                            s4v.inc(i4)

                        ln_cols(s4v, s4s, D)

                        @block.vector
                        def _(vector, t1=t1, ti=ti, pb=pb):
                            cc = r_cols.f32
                            vector.wait_ge(s4st[t1 % 2].h, 16 * (t1 // 2 + 1))
                            vector.wait_ge(s4x1.h, 16 * (t1 + 1))
                            if t1 > 0:
                                vector.wait_ge(s4y.h, 16 * t1)
                            vector.tensor_scalar(r_yst.f32[:, 0:1024], r_sst[t1 % 2].f32,
                                                 cc[:, 4:5], cc[:, 5:6], OP.mult, OP.add)
                            i2 = vector.tensor_scalar(r_yst.f32[:, 1024:2048],
                                                      bankspan(pb, 2),
                                                      cc[:, 4:5], cc[:, 5:6],
                                                      OP.mult, OP.add)
                            v4ps[ti] = s4v.inc(i2)
                            vector.wait_ge(s4v.h, s4v.v)
                            i3 = vector.tensor_tensor(r_yst.f32, r_yst.f32, r_x1rd.f32,
                                                      OP.add)
                            s4v.inc(i3)

                        @block.scalar
                        def _(scalar, t1=t1):
                            scalar.wait_ge(s4v.h, s4v.v)
                            s4y.inc(scalar.dma_start(out=y[128 * t1:128 * (t1 + 1), :],
                                                     in_=r_yst.f32), 16)

        @block.sync
        def _(sync):
            for _s in _sems.values():
                if _s.v > 0:
                    sync.wait_ge(_s.h, _s.v)

    return nc


def _host_pack(Wq, Wk, Wv, Wo, w1, w2, w3):
    wqkvT = np.concatenate([np.asarray(Wq, np.float32), np.asarray(Wk, np.float32),
                            np.asarray(Wv, np.float32)], 0)
    wqkv_b = np.ascontiguousarray(
        wqkvT.T.reshape(16, 128, NO, 128).transpose(2, 0, 1, 3))
    wo_t = np.ascontiguousarray(np.asarray(Wo, np.float32).T)
    w1t = np.asarray(w1, np.float32).T
    w2t = np.asarray(w2, np.float32).T
    w1bb = np.ascontiguousarray(w1t.reshape(D, NHC, 128).transpose(1, 0, 2))
    w2bb = np.ascontiguousarray(w2t.reshape(D, NHC, 128).transpose(1, 0, 2))
    w3t = np.asarray(w3, np.float32).T
    w3bb = np.ascontiguousarray(w3t.reshape(NHC, 128, 2, 1024).transpose(2, 0, 1, 3))
    return wqkv_b, wo_t, w1bb, w2bb, w3bb


def _tables():
    inv = 1.0 / (10000.0 ** (np.arange(0, HD, 2, dtype=np.float64) / HD))
    t = np.arange(S, dtype=np.float64)
    fr = np.outer(t, inv)
    emb = np.concatenate([fr, fr], -1)
    cos = np.cos(emb).T.astype(BF)
    sin = np.sin(emb).T.astype(np.float32)
    sin[:64] *= -1.0
    sinx_t = sin.astype(BF)
    maskut = np.triu(np.ones((C, C), np.float32))
    mask4m = np.tile(maskut, (1, 4)).astype(BF)
    identm = np.eye(128, dtype=np.float32).astype(BF)
    return (np.ascontiguousarray(cos), np.ascontiguousarray(sinx_t),
            np.ascontiguousarray(mask4m), np.ascontiguousarray(identm))


def kernel(L0_attn_x, L0_attn_Wq, L0_attn_Wk, L0_attn_Wv, L0_attn_Wo,
           L0_ffn_w1, L0_ffn_w2, L0_ffn_w3):
    x = np.ascontiguousarray(np.asarray(L0_attn_x, np.float32))
    wqkv_b, wo_t, w1bb, w2bb, w3bb = _host_pack(L0_attn_Wq, L0_attn_Wk, L0_attn_Wv,
                                                L0_attn_Wo, L0_ffn_w1, L0_ffn_w2,
                                                L0_ffn_w3)
    cos, sinx_t, mask4m, identm = _tables()
    nc = build_nc()
    shared = dict(wqkv=wqkv_b, wo=wo_t, w1b=w1bb, w2b=w2bb, w3b=w3bb,
                  cosf=cos, sinx=sinx_t, mask4=mask4m, ident=identm)
    in_maps = [dict(x0=np.ascontiguousarray(x[c]), **shared) for c in range(8)]
    res = run_bass_kernel_spmd(nc, in_maps, core_ids=list(range(8)))
    return np.stack([np.asarray(r["y"], np.float32) for r in res.results], 0)


# version 03 (final)
# speedup vs baseline: 1.0421x; 1.0281x over previous
import numpy as np
import ml_dtypes
from contextlib import ExitStack

import concourse.bass as bass
import concourse.mybir as mybir
from concourse.bass_utils import run_bass_kernel_spmd
from concourse.alu_op_type import AluOpType as OP

F32 = mybir.dt.float32
BF16 = mybir.dt.bfloat16
AF = mybir.ActivationFunctionType
AXL = mybir.AxisListType
BF = ml_dtypes.bfloat16

S = 2048
D = 2048
H, KVH, HD = 16, 4, 128
NO = 24
C = 128
NCH = S // C
HFF = 5632
NHC = HFF // 128
EPS_LN = 1e-5
EPS_ATT = 1e-6
EPS_Q = 1e-8
NW = HFF * D
GSTRIP = 512
NGS = NW // (128 * GSTRIP)
NGTOT = 3 * NGS

ARENA_F32 = 52984

PHASES = ["p0", "p1", "p2a", "p2b", "gamma", "xn2t", "p3", "p4", "end"]


class Reg:
    def __init__(self, arena, off, cols, name):
        self.arena, self.off, self.cols, self.name = arena, off, cols, name

    @property
    def f32(self):
        return self.arena.ap[:, self.off:self.off + self.cols]

    @property
    def b16(self):
        return self.arena.ap[:, self.off:self.off + self.cols].bitcast(BF16)

    def free(self):
        self.arena.free(self)


class Arena:
    def __init__(self, ap):
        self.ap = ap
        self.avail = [(0, ARENA_F32)]

    def alloc(self, cols, name=""):
        cols = (cols + 7) & ~7
        for i, (o, c) in enumerate(self.avail):
            if c >= cols:
                self.avail[i] = (o + cols, c - cols)
                if self.avail[i][1] == 0:
                    self.avail.pop(i)
                return Reg(self, o, cols, name)
        raise RuntimeError(f"arena OOM for {name}: need {cols} f32 cols, free={self.avail}")

    def alloc_b16(self, cols_b16, name=""):
        return self.alloc((cols_b16 + 1) // 2, name)

    def free(self, reg):
        self.avail.append((reg.off, reg.cols))
        self.avail.sort()
        merged = []
        for o, c in self.avail:
            if merged and merged[-1][0] + merged[-1][1] == o:
                merged[-1] = (merged[-1][0], merged[-1][1] + c)
            else:
                merged.append((o, c))
        self.avail = merged


class Sem:
    def __init__(self, nc, ctx, name):
        self.h = ctx.enter_context(nc.semaphore(name))
        self.v = 0

    def inc(self, instr, n=1):
        instr.then_inc(self.h, n)
        self.v += n
        return self.v


def build_nc(dbg=(), stop_after=None, p0_chunks=NCH, gamma_on=True, p4_rounds=2, p4_parts=7):
    dbg = set(dbg)
    _sa = PHASES.index(stop_after) if stop_after else PHASES.index("end")

    def live(ph):
        return PHASES.index(ph) <= _sa

    nc = bass.Bass("TRN2", target_bir_lowering=False)

    x0 = nc.dram_tensor("x0", [S, D], F32, kind="ExternalInput").ap()
    wqkv = nc.dram_tensor("wqkv", [NO, 16, 128, 128], F32, kind="ExternalInput").ap()
    wo = nc.dram_tensor("wo", [D, D], F32, kind="ExternalInput").ap()
    w1b = nc.dram_tensor("w1b", [NHC, D, 128], F32, kind="ExternalInput").ap()
    w2b = nc.dram_tensor("w2b", [NHC, D, 128], F32, kind="ExternalInput").ap()
    w3b = nc.dram_tensor("w3b", [2, NHC, 128, 1024], F32, kind="ExternalInput").ap()
    cosf = nc.dram_tensor("cosf", [128, S], BF16, kind="ExternalInput").ap()
    sinx = nc.dram_tensor("sinx", [128, S], BF16, kind="ExternalInput").ap()
    mask4 = nc.dram_tensor("mask4", [128, 512], BF16, kind="ExternalInput").ap()
    ident = nc.dram_tensor("ident", [128, 128], BF16, kind="ExternalInput").ap()
    y = nc.dram_tensor("y", [S, D], F32, kind="ExternalOutput").ap()
    x1s = nc.dram_tensor("x1s", [S, D], F32, kind="Internal").ap()
    hids = nc.dram_tensor("hids", [NHC, 128, S], BF16, kind="Internal").ap()
    stash = nc.dram_tensor("stash", [NCH, 128, 1024], F32, kind="Internal").ap()
    gsc = nc.dram_tensor("gsc", [12, 128], F32, kind="Internal").ap()

    dbg_t = {}

    def dbg_ap(name, shape, dt=F32):
        dbg_t[name] = nc.dram_tensor("dbg_" + name, shape, dt, kind="ExternalOutput").ap()
        return dbg_t[name]

    with ExitStack() as ctx:
        big = ctx.enter_context(nc.sbuf_tensor("arena", [128, ARENA_F32], F32))
        A = Arena(big[:])
        PSA = ctx.enter_context(nc.psum_tensor("psa", [128, 2048], F32))
        PSB = ctx.enter_context(nc.psum_tensor("psb", [128, 2048], F32))

        def bank(k, off=0, n=512):
            t = PSA if k < 4 else PSB
            return t[:, 512 * (k % 4) + off: 512 * (k % 4) + off + n]

        def bankspan(k, nb):
            t = PSA if k < 4 else PSB
            return t[:, 512 * (k % 4): 512 * (k % 4) + 512 * nb]

        block = ctx.enter_context(nc.Block())
        _sems = {}

        def sem(name):
            if name not in _sems:
                _sems[name] = Sem(nc, ctx, name)
            return _sems[name]

        vch = sem("vch")
        sch = sem("sch")

        def vbar(vector, instr):
            vch.inc(instr)
            vector.wait_ge(vch.h, vch.v)

        def sbar(scalar, instr):
            sch.inc(instr)
            scalar.wait_ge(sch.h, sch.v)

        r_cos = A.alloc_b16(S, "cos")
        r_sin = A.alloc_b16(S, "sinx")
        r_id = A.alloc_b16(128, "ident")
        r_cols = A.alloc(16, "lncols")
        r_gsum = A.alloc(8, "gsum")
        r_tq = A.alloc(8, "tq")
        r_const = A.alloc(8, "const")
        r_trash = A.alloc_b16(1024, "trash")
        nc.const_aps.register(F32, -0.5, r_const.f32[:, 0:1])
        identb = r_id.b16
        trashA = r_trash.b16[:, 0:1024]

        s_fix = sem("fix")

        @block.sync
        def _(sync):
            s_fix.inc(sync.dma_start(out=r_cos.b16, in_=cosf), 16)
            s_fix.inc(sync.dma_start(out=r_sin.b16, in_=sinx), 16)
            s_fix.inc(sync.dma_start(out=r_id.b16, in_=ident), 16)

        @block.vector
        def _(vector):
            vector.memset(r_const.f32[:, 0:1], -0.5)
            i2 = vector.memset(r_gsum.f32, 0.0)
            vbar(vector, i2)
            vector.wait_ge(s_fix.h, s_fix.v)

        r_gs = [A.alloc(GSTRIP, "gs0"), A.alloc(GSTRIP, "gs1")]
        gflat = [w1b.rearrange("a b c -> (a b c)"), w2b.rearrange("a b c -> (a b c)"),
                 w3b.rearrange("a b c d -> (a b c d)")]
        s_gld = [sem("gld0"), sem("gld1")]
        s_gvs = sem("gvs")
        _gi = [0]

        def emit_gamma(n):
            j0 = _gi[0]
            n = min(n, NGTOT - j0)
            if n <= 0:
                return
            _gi[0] += n

            @block.sync
            def _(sync):
                for j in range(j0, j0 + n):
                    pp = j % 2
                    if j >= 2:
                        sync.wait_ge(s_gvs.h, 2 * (j - 1))
                    w, js = j // NGS, j % NGS
                    src = gflat[w][128 * GSTRIP * js: 128 * GSTRIP * (js + 1)]
                    s_gld[pp].inc(sync.dma_start(
                        out=r_gs[pp].f32, in_=src.rearrange("(p f) -> p f", f=GSTRIP)), 16)

            @block.vector
            def _(vector):
                for j in range(j0, j0 + n):
                    pp = j % 2
                    w = j // NGS
                    vector.wait_ge(s_gld[pp].h, 16 * (j // 2 + 1))
                    i1 = vector.reduce_sum(r_gsum.f32[:, 4 + pp:5 + pp], r_gs[pp].f32,
                                           axis=AXL.X, apply_absolute_value=True)
                    s_gvs.inc(i1)
                    vector.wait_ge(s_gvs.h, s_gvs.v)
                    i2 = vector.tensor_tensor(r_gsum.f32[:, w:w + 1], r_gsum.f32[:, w:w + 1],
                                              r_gsum.f32[:, 4 + pp:5 + pp], OP.add)
                    s_gvs.inc(i2)
                    vector.wait_ge(s_gvs.h, s_gvs.v)

        def ln_cols(s_v, s_s, nfeat, rscale=None):
            @block.vector
            def _(vector):
                cc = r_cols.f32
                vector.wait_ge(s_v.h, s_v.v)
                vector.tensor_scalar(cc[:, 2:3], cc[:, 0:1], 1.0 / nfeat, None, OP.mult)
                i1 = vector.tensor_scalar(cc[:, 3:4], cc[:, 1:2], 1.0 / nfeat, None, OP.mult)
                vbar(vector, i1)
                i2 = vector.tensor_tensor(cc[:, 6:7], cc[:, 2:3], cc[:, 2:3], OP.mult)
                vbar(vector, i2)
                i3 = vector.tensor_tensor(cc[:, 3:4], cc[:, 3:4], cc[:, 6:7], OP.subtract)
                vbar(vector, i3)
                i4 = vector.tensor_scalar(cc[:, 3:4], cc[:, 3:4], EPS_LN, None, OP.add)
                s_v.inc(i4)

            @block.scalar
            def _(scalar):
                cc = r_cols.f32
                scalar.wait_ge(s_v.h, s_v.v)
                i1 = scalar.activation(cc[:, 4:5], cc[:, 3:4], AF.Ln)
                sbar(scalar, i1)
                i2 = scalar.activation(cc[:, 4:5], cc[:, 4:5], AF.Exp, scale=-0.5)
                s_s.inc(i2)

            @block.vector
            def _(vector):
                cc = r_cols.f32
                vector.wait_ge(s_s.h, s_s.v)
                if rscale is not None:
                    ir = vector.tensor_scalar(cc[:, 4:5], cc[:, 4:5], rscale, None, OP.mult)
                    vbar(vector, ir)
                i5 = vector.scalar_tensor_tensor(cc[:, 5:6], cc[:, 2:3], -1.0, cc[:, 4:5],
                                                 OP.mult, OP.mult)
                s_v.inc(i5)
                vector.wait_ge(s_v.h, s_v.v)

        r_xn1t = A.alloc_b16(16 * S, "xn1t")
        xn1t3 = r_xn1t.b16.rearrange("p (dc t) -> p dc t", t=S)
        r_x0 = A.alloc(D, "x0buf")
        r_stg = A.alloc_b16(D, "xstage")
        s_x0 = sem("x0ld")
        s_p0v = sem("p0v")
        s_p0s = sem("p0s")
        s_p0t = sem("p0t")
        s_p0e = sem("p0e")

        @block.sync
        def _(sync):
            s_x0.inc(sync.dma_start(out=r_x0.f32, in_=x0[0:128, :]), 16)

        for i in range(p0_chunks):
            @block.scalar
            def _(scalar, i=i):
                scalar.wait_ge(s_x0.h, 16 * (i + 1))
                if i > 0:
                    scalar.wait_ge(s_p0v.h, s_p0v.v)
                    scalar.wait_ge(s_p0t.h, s_p0t.v)
                scalar.activation(trashA, r_x0.f32[:, 0:1024], AF.Square)
                i2 = scalar.activation(r_stg.b16[:, 0:1024], r_x0.f32[:, 1024:2048],
                                       AF.Square)
                s_p0s.inc(i2)

            @block.vector
            def _(vector, i=i):
                vector.wait_ge(s_x0.h, 16 * (i + 1))
                vector.reduce_sum(r_cols.f32[:, 0:1], r_x0.f32, axis=AXL.X)
                vector.wait_ge(s_p0s.h, s_p0s.v)
                vector.reduce_sum(r_cols.f32[:, 6:7], trashA, axis=AXL.X)
                i3 = vector.reduce_sum(r_cols.f32[:, 1:2], r_stg.b16[:, 0:1024], axis=AXL.X)
                vbar(vector, i3)
                i4 = vector.tensor_tensor(r_cols.f32[:, 1:2], r_cols.f32[:, 1:2],
                                          r_cols.f32[:, 6:7], OP.add)
                s_p0v.inc(i4)

            ln_cols(s_p0v, s_p0s, D)

            @block.vector
            def _(vector, i=i):
                cc = r_cols.f32
                if i > 0:
                    vector.wait_ge(s_p0t.h, s_p0t.v)
                i1 = vector.tensor_scalar(r_stg.b16, r_x0.f32, cc[:, 4:5], cc[:, 5:6],
                                          OP.mult, OP.add)
                s_p0v.inc(i1)

            if i + 1 < NCH:
                @block.sync
                def _(sync, i=i):
                    sync.wait_ge(s_p0v.h, s_p0v.v)
                    sync.wait_ge(s_p0s.h, s_p0s.v)
                    s_x0.inc(sync.dma_start(out=r_x0.f32,
                                            in_=x0[128 * (i + 1):128 * (i + 2), :]), 16)

            for g in range(4):
                bk = g % 2

                @block.tensor
                def _(tensor, i=i, g=g, bk=bk):
                    tensor.wait_ge(s_p0v.h, s_p0v.v)
                    tensor.wait_ge(s_fix.h, s_fix.v)
                    gidx = i * 4 + g
                    if gidx >= 2:
                        tensor.wait_ge(s_p0e.h, gidx - 1)
                    for d4 in range(4):
                        d = 4 * g + d4
                        mm = tensor.matmul(bank(bk, 128 * d4, 128),
                                           r_stg.b16[:, 128 * d:128 * (d + 1)],
                                           identb, start=(d4 == 0), stop=(d4 == 3))
                    s_p0t.inc(mm)

                @block.vector
                def _(vector, i=i, g=g, bk=bk):
                    vector.wait_ge(s_p0t.h, s_p0t.v)
                    cp = vector.tensor_copy(
                        xn1t3[:, 4 * g:4 * g + 4, 128 * i:128 * (i + 1)],
                        bankspan(bk, 1).rearrange("p (a b) -> p a b", b=128))
                    s_p0e.inc(cp)

            if gamma_on:
                emit_gamma(4)

        if "xn1t" in dbg:
            s_d0 = sem("d0")

            @block.scalar
            def _(scalar):
                scalar.wait_ge(s_p0e.h, s_p0e.v)
                s_d0.inc(scalar.dma_start(out=dbg_ap("xn1t", [128, 16 * S], BF16),
                                          in_=r_xn1t.b16), 16)

        if live("p1"):
            r_x0.free()
            r_stg.free()
            r_qf = A.alloc_b16(H * S, "qf")
            r_kf = A.alloc_b16(KVH * S, "kf")
            r_vf = A.alloc_b16(KVH * S, "vf")
            r_wsl = A.alloc(D, "wslab")
            r_wb = [A.alloc_b16(D, "wb0"), A.alloc_b16(D, "wb1")]
            r_pst = A.alloc_b16(D, "p1stage")
            r_sh = A.alloc_b16(D, "p1shift")
            r_ra = A.alloc_b16(D, "p1a")
            r_rb = A.alloc_b16(D, "p1b")

            s_wld = sem("wld")
            s_wc = sem("wc")
            s_p1t = sem("p1t")
            s_p1c = sem("p1c")
            s_sh = sem("shd")
            s_rva = sem("rva")
            s_rse = sem("rse")
            s_rvq = sem("rvq")

            ROPE_OCS = 20

            for oc in range(NO):
                ri = min(oc, ROPE_OCS)

                @block.sync
                def _(sync, oc=oc):
                    if oc == 0:
                        sync.wait_ge(s_p0e.h, s_p0e.v)
                        sync.wait_ge(s_p0v.h, s_p0v.v)
                        sync.wait_ge(s_p0s.h, s_p0s.v)
                    else:
                        sync.wait_ge(s_wc.h, oc)
                    s_wld.inc(sync.dma_start(
                        out=r_wsl.f32.rearrange("p (a b) -> p a b", b=128),
                        in_=wqkv[oc].rearrange("a p b -> p a b")), 16)

                @block.scalar
                def _(scalar, oc=oc):
                    scalar.wait_ge(s_wld.h, 16 * (oc + 1))
                    if oc >= 2:
                        scalar.wait_ge(s_p1t.h, oc - 1)
                    s_wc.inc(scalar.copy(r_wb[oc % 2].b16, r_wsl.f32))

                @block.tensor
                def _(tensor, oc=oc):
                    tensor.wait_ge(s_wc.h, oc + 1)
                    if oc == 0:
                        tensor.wait_ge(s_p0e.h, s_p0e.v)
                    if oc >= 2:
                        tensor.wait_ge(s_p1c.h, oc - 1)
                    pb = 0 if oc % 2 == 0 else 4
                    for d in range(16):
                        for tb in range(4):
                            mm = tensor.matmul(
                                bank(pb + tb),
                                r_wb[oc % 2].b16[:, 128 * d:128 * (d + 1)],
                                r_xn1t.b16[:, 2048 * d + 512 * tb: 2048 * d + 512 * (tb + 1)],
                                start=(d == 0), stop=(d == 15))
                    s_p1t.inc(mm)

                @block.scalar
                def _(scalar, oc=oc, ri=ri):
                    scalar.wait_ge(s_p1t.h, oc + 1)
                    pb = 0 if oc % 2 == 0 else 4
                    if oc < ROPE_OCS:
                        if ri > 0:
                            scalar.wait_ge(s_rva.h, 2 * ri)
                            scalar.wait_ge(s_sh.h, 32 * ri)
                        cp = scalar.copy(r_pst.b16, bankspan(pb, 4))
                        s_p1c.inc(cp)
                        scalar.wait_ge(s_p1c.h, s_p1c.v)
                        if ri > 0:
                            scalar.wait_ge(s_rvq.h, ri)
                        s_sh.inc(scalar.dma_start(out=r_sh.b16[0:64, :],
                                                  in_=r_pst.b16[64:128, :]), 16)
                        s_sh.inc(scalar.dma_start(out=r_sh.b16[64:128, :],
                                                  in_=r_pst.b16[0:64, :]), 16)
                    else:
                        g = oc - 20
                        s_p1c.inc(scalar.copy(r_vf.b16[:, 2048 * g:2048 * (g + 1)],
                                              bankspan(pb, 4)))

                if oc < ROPE_OCS:
                    @block.vector
                    def _(vector, oc=oc, ri=ri):
                        vector.wait_ge(s_p1c.h, oc + 1)
                        if ri > 0:
                            vector.wait_ge(s_rvq.h, ri)
                        ia = vector.tensor_tensor(r_ra.b16, r_pst.b16, r_cos.b16, OP.mult)
                        s_rva.inc(ia)
                        vector.wait_ge(s_sh.h, 32 * (ri + 1))
                        ib = vector.tensor_tensor(r_rb.b16, r_sh.b16, r_sin.b16, OP.mult)
                        vbar(vector, ib)
                        ir = vector.tensor_tensor(r_ra.b16, r_ra.b16, r_rb.b16, OP.add)
                        vbar(vector, ir)
                        im = vector.tensor_scalar(r_rb.b16, r_ra.b16, 0.0, None, OP.min)
                        s_rva.inc(im)

                    @block.scalar
                    def _(scalar, ri=ri):
                        scalar.wait_ge(s_rva.h, 2 * ri + 2)
                        s_rse.inc(scalar.activation(r_rb.b16, r_rb.b16, AF.Exp))

                    @block.vector
                    def _(vector, oc=oc, ri=ri):
                        vector.wait_ge(s_rse.h, ri + 1)
                        if oc < 16:
                            dst = r_qf.b16[:, 2048 * oc:2048 * (oc + 1)]
                        else:
                            dst = r_kf.b16[:, 2048 * (oc - 16):2048 * (oc - 15)]
                        s_rvq.inc(vector.scalar_tensor_tensor(dst, r_ra.b16, 0.0, r_rb.b16,
                                                              OP.max, OP.add))
                if gamma_on:
                    emit_gamma(8)

            if "qf" in dbg:
                s_d1 = sem("d1")

                @block.scalar
                def _(scalar):
                    scalar.wait_ge(s_rvq.h, s_rvq.v)
                    scalar.wait_ge(s_p1c.h, s_p1c.v)
                    s_d1.inc(scalar.dma_start(out=dbg_ap("qf", [128, H * S], BF16),
                                              in_=r_qf.b16), 16)
                    s_d1.inc(scalar.dma_start(out=dbg_ap("kf", [128, KVH * S], BF16),
                                              in_=r_kf.b16), 16)
                    s_d1.inc(scalar.dma_start(out=dbg_ap("vf", [128, KVH * S], BF16),
                                              in_=r_vf.b16), 16)

            for r in (r_wsl, r_wb[0], r_wb[1], r_pst, r_sh, r_ra, r_rb, r_xn1t,
                      r_cos, r_sin):
                r.free()

        if live("p2a"):
            r_attb = A.alloc_b16(NCH * 2048, "attb")
            r_mask = A.alloc_b16(512, "mask4")
            r_ktk = A.alloc_b16(512, "ktok")
            r_vag = A.alloc_b16(4 * 132, "vaug")
            r_smt = A.alloc_b16(512, "smt")
            r_sta = A.alloc_b16(4 * 132, "stateaug")
            r_rec = A.alloc(8, "recden")

            s_mld = sem("mld")
            s2t = sem("s2t")
            s2v = sem("s2v")
            p1_c_done0 = s_p1c.v
            p1_q_done0 = s_rvq.v
            p1_sh_done0 = s_sh.v
            p1_wc_done0 = s_wc.v

            @block.sync
            def _(sync):
                sync.wait_ge(s_rvq.h, p1_q_done0)
                sync.wait_ge(s_p1c.h, p1_c_done0)
                sync.wait_ge(s_sh.h, p1_sh_done0)
                sync.wait_ge(s_wc.h, p1_wc_done0)
                s_mld.inc(sync.dma_start(out=r_mask.b16, in_=mask4), 16)

            @block.vector
            def _(vector):
                vector.wait_ge(s_sh.h, p1_sh_done0)
                m0 = vector.memset(r_vag.f32, 0.0)
                vbar(vector, m0)
                for g in range(4):
                    mi = vector.memset(r_vag.b16[:, 132 * g + 128:132 * g + 129], 1.0)
                s2v.inc(mi)
            v_vag_ones = s2v.v

            @block.tensor
            def _(tensor):
                tensor.wait_ge(s_fix.h, s_fix.v)
                tensor.matmul(bank(0, 500, 1), identb, identb[:, 0:1], start=True, stop=True)
                s2t.inc(tensor.matmul(bank(1, 500, 1), identb, identb[:, 0:1],
                                      start=True, stop=True))

            p1_c_done = s_p1c.v
            p1_q_done = s_rvq.v

            t_trans = {}
            v_tevac = {}
            t_scores = {}
            v_mask = {}
            t_att = {}
            v_atte = {}
            t_supd = [0]
            v_sevac = [0]

            for i in range(NCH):
                @block.tensor
                def _(tensor, i=i):
                    if i == 0:
                        tensor.wait_ge(s_rvq.h, p1_q_done)
                        tensor.wait_ge(s_p1c.h, p1_c_done)
                        tensor.wait_ge(s2v.h, v_vag_ones)
                    else:
                        tensor.wait_ge(s2v.h, v_mask[(i - 1, 3)])
                    for g in range(4):
                        mm = tensor.matmul(
                            bank(2, 128 * g, 128),
                            r_kf.b16[:, 2048 * g + 128 * i:2048 * g + 128 * (i + 1)],
                            identb, start=(g == 0), stop=(g == 3))
                    for g in range(4):
                        mm = tensor.matmul(
                            bank(3, 128 * g, 128),
                            r_vf.b16[:, 2048 * g + 128 * i:2048 * g + 128 * (i + 1)],
                            identb, start=(g == 0), stop=(g == 3))
                    t_trans[i] = s2t.inc(mm)

                @block.vector
                def _(vector, i=i):
                    vector.wait_ge(s2t.h, t_trans[i])
                    if i == 0:
                        vector.wait_ge(s_mld.h, 16)
                    vector.tensor_copy(r_ktk.b16, bankspan(2, 1))
                    cp = vector.tensor_copy(
                        r_vag.b16.rearrange("p (a b) -> p a b", b=132)[:, :, 0:128],
                        bankspan(3, 1).rearrange("p (a b) -> p a b", b=128))
                    s2v.inc(cp)
                    v_tevac[i] = s2v.v

                for g in range(4):
                    sb = 2 + (g % 2)

                    @block.tensor
                    def _(tensor, i=i, g=g, sb=sb):
                        tensor.wait_ge(s2v.h, v_tevac[i])
                        if g >= 2:
                            tensor.wait_ge(s2v.h, v_mask[(i, g - 2)])
                        for j in range(4):
                            h = 4 * g + j
                            mm = tensor.matmul(
                                bank(sb, 128 * j, 128),
                                r_kf.b16[:, 2048 * g + 128 * i:2048 * g + 128 * (i + 1)],
                                r_qf.b16[:, 2048 * h + 128 * i:2048 * h + 128 * (i + 1)],
                                start=(j == 0), stop=(j == 3))
                        t_scores[(i, g)] = s2t.inc(mm)

                    @block.vector
                    def _(vector, i=i, g=g, sb=sb):
                        vector.wait_ge(s2t.h, t_scores[(i, g)])
                        if g > 0:
                            vector.wait_ge(s2t.h, t_att[(i, 4 * g - 1)])
                        elif i > 0:
                            vector.wait_ge(s2t.h, t_att[(i - 1, 15)])
                        s2v.inc(vector.tensor_tensor(r_smt.b16, bankspan(sb, 1),
                                                     r_mask.b16, OP.mult))
                        v_mask[(i, g)] = s2v.v

                    for j in range(4):
                        h = 4 * g + j
                        ab = 4 + (h % 2)

                        @block.tensor
                        def _(tensor, i=i, g=g, j=j, h=h, ab=ab):
                            tensor.wait_ge(s2v.h, v_mask[(i, g)])
                            if h >= 2:
                                tensor.wait_ge(s2v.h, v_atte[(i, h - 2)])
                            elif i > 0:
                                tensor.wait_ge(s2v.h, v_atte[(i - 1, 14 + h)])
                            mm = tensor.matmul(bank(ab, 0, 129),
                                               r_smt.b16[:, 128 * j:128 * (j + 1)],
                                               r_vag.b16[:, 132 * g:132 * g + 129],
                                               start=True, stop=(i == 0))
                            if i > 0:
                                mm = tensor.matmul(
                                    bank(ab, 0, 129),
                                    r_qf.b16[:, 2048 * h + 128 * i:2048 * h + 128 * (i + 1)],
                                    r_sta.b16[:, 132 * g:132 * g + 129],
                                    start=False, stop=True)
                            t_att[(i, h)] = s2t.inc(mm)

                        @block.vector
                        def _(vector, i=i, h=h, ab=ab):
                            vector.wait_ge(s2t.h, t_att[(i, h)])
                            cc = r_rec.f32[:, (h % 2):(h % 2) + 1]
                            i1 = vector.tensor_scalar(cc, bank(ab, 128, 1), EPS_ATT,
                                                      None, OP.add)
                            vbar(vector, i1)
                            i2 = vector.reciprocal(cc, cc)
                            vbar(vector, i2)
                            i3 = vector.tensor_scalar(
                                r_attb.b16[:, 2048 * i + 128 * h:2048 * i + 128 * (h + 1)],
                                bank(ab, 0, 128), cc, None, OP.mult)
                            s2v.inc(i3)
                            v_atte[(i, h)] = s2v.v

                if i < NCH - 1:
                    @block.tensor
                    def _(tensor, i=i):
                        if i > 0:
                            tensor.wait_ge(s2v.h, v_sevac[0])
                        for g in range(4):
                            mm = tensor.matmul(bank(g // 2, 140 * (g % 2), 129),
                                               r_ktk.b16[:, 128 * g:128 * (g + 1)],
                                               r_vag.b16[:, 132 * g:132 * g + 129],
                                               start=False, stop=False,
                                               skip_group_check=True)
                        t_supd[0] = s2t.inc(mm)

                    @block.vector
                    def _(vector, i=i):
                        vector.wait_ge(s2t.h, t_supd[0])
                        for g in range(4):
                            cp = vector.tensor_copy(r_sta.b16[:, 132 * g:132 * g + 129],
                                                    bank(g // 2, 140 * (g % 2), 129))
                        v_sevac[0] = s2v.inc(cp)

                if gamma_on:
                    emit_gamma(11)

            if "attb" in dbg:
                s_d2 = sem("d2")

                @block.scalar
                def _(scalar):
                    scalar.wait_ge(s2v.h, s2v.v)
                    s_d2.inc(scalar.dma_start(
                        out=dbg_ap("attb", [S, H * HD], BF16).rearrange(
                            "(i p) o -> p i o", p=128),
                        in_=r_attb.b16.rearrange("p (i o) -> p i o", o=2048)), 16)

            for r in (r_ktk, r_vag, r_smt, r_sta, r_rec, r_mask, r_qf, r_kf, r_vf):
                r.free()
            p2a_v_final = s2v.v

        if live("p2b"):
            r_att_t = [A.alloc_b16(2048, "attT0"), A.alloc_b16(2048, "attT1")]
            r_wotb = A.alloc_b16(16 * 2048, "wotb")
            r_wosl = A.alloc(D, "woslab")
            r_x0b = A.alloc(D, "x0re")
            r_x1b = A.alloc(D, "x1buf")

            s_wold = sem("wold")
            s_woc = sem("woc")
            s_x0b = sem("x0re")
            s_x1w = sem("x1w")
            s2bt = sem("s2bt")
            s2bv = sem("s2bv")
            s2bs = sem("s2bs")

            @block.sync
            def _(sync):
                sync.wait_ge(s2v.h, p2a_v_final)
                s_x0b.inc(sync.dma_start(out=r_x0b.f32, in_=x0[0:128, :]), 16)

            for oc in range(16):
                @block.sync
                def _(sync, oc=oc):
                    if oc > 0:
                        sync.wait_ge(s_woc.h, oc)
                    s_wold.inc(sync.dma_start(out=r_wosl.f32,
                                              in_=wo[128 * oc:128 * (oc + 1), :]), 16)

                @block.scalar
                def _(scalar, oc=oc):
                    scalar.wait_ge(s_wold.h, 16 * (oc + 1))
                    s_woc.inc(scalar.copy(r_wotb.b16[:, 2048 * oc:2048 * (oc + 1)],
                                          r_wosl.f32))

            v_ate = {}
            t_wo = {}
            v_x1ps = {}
            v_resid = {}
            v_xn2 = {}

            for i in range(NCH):
                for g in range(4):
                    ab = 4 + (g % 2)

                    @block.tensor
                    def _(tensor, i=i, g=g, ab=ab):
                        if i == 0 and g == 0:
                            tensor.wait_ge(s2v.h, p2a_v_final)
                        gid = 4 * i + g
                        if gid >= 2:
                            prev = (i, g - 2) if g >= 2 else (i - 1, g + 2)
                            tensor.wait_ge(s2bv.h, v_ate[prev])
                        for d4 in range(4):
                            oc = 4 * g + d4
                            mm = tensor.matmul(
                                bank(ab, 128 * d4, 128),
                                r_attb.b16[:, 2048 * i + 128 * oc:2048 * i + 128 * (oc + 1)],
                                identb, start=(d4 == 0), stop=(d4 == 3))
                        s2bt.inc(mm)

                    @block.vector
                    def _(vector, i=i, g=g, ab=ab):
                        vector.wait_ge(s2bt.h, s2bt.v)
                        att3 = r_att_t[i % 2].b16.rearrange("p (a b) -> p a b", b=128)
                        cp = vector.tensor_copy(
                            att3[:, 4 * g:4 * g + 4, :],
                            bankspan(ab, 1).rearrange("p (a b) -> p a b", b=128))
                        s2bv.inc(cp)
                        v_ate[(i, g)] = s2bv.v

                @block.tensor
                def _(tensor, i=i):
                    tensor.wait_ge(s2bv.h, v_ate[(i, 3)])
                    if i > 0:
                        tensor.wait_ge(s2bv.h, v_x1ps[i - 1])
                        tensor.wait_ge(s2bs.h, s2bs.v)
                    for oc in range(16):
                        if i == 0:
                            tensor.wait_ge(s_woc.h, oc + 1)
                        for b in range(4):
                            mm = tensor.matmul(
                                bank(b), r_att_t[i % 2].b16[:, 128 * oc:128 * (oc + 1)],
                                r_wotb.b16[:, 2048 * oc + 512 * b:2048 * oc + 512 * (b + 1)],
                                start=(oc == 0), stop=(oc == 15))
                    t_wo[i] = s2bt.inc(mm)

                @block.vector
                def _(vector, i=i):
                    vector.wait_ge(s2bt.h, t_wo[i])
                    i1 = vector.reduce_sum(r_cols.f32[:, 0:1], PSA[:, :], axis=AXL.X)
                    s2bv.inc(i1)

                v_worad = s2bv.v

                @block.scalar
                def _(scalar, i=i, v_worad=v_worad):
                    scalar.wait_ge(s2bv.h, v_worad)
                    if i > 0:
                        scalar.wait_ge(s_x1w.h, 16 * i)
                    scalar.activation(trashA, PSA[:, 0:1024], AF.Square)
                    i2 = scalar.activation(r_x1b.b16[:, 0:1024], PSA[:, 1024:2048],
                                           AF.Square)
                    s2bs.inc(i2)

                @block.vector
                def _(vector, i=i):
                    vector.wait_ge(s2bs.h, s2bs.v)
                    vector.reduce_sum(r_cols.f32[:, 6:7], trashA, axis=AXL.X)
                    i3 = vector.reduce_sum(r_cols.f32[:, 1:2], r_x1b.b16[:, 0:1024],
                                           axis=AXL.X)
                    vbar(vector, i3)
                    i4 = vector.tensor_tensor(r_cols.f32[:, 1:2], r_cols.f32[:, 1:2],
                                              r_cols.f32[:, 6:7], OP.add)
                    s2bv.inc(i4)

                ln_cols(s2bv, s2bs, D)

                @block.vector
                def _(vector, i=i):
                    cc = r_cols.f32
                    if i > 0:
                        vector.wait_ge(s_x1w.h, 16 * i)
                        vector.wait_ge(s2bv.h, v_xn2[i - 1])
                    i1 = vector.tensor_scalar(r_x1b.f32, PSA[:, :], cc[:, 4:5], cc[:, 5:6],
                                              OP.mult, OP.add)
                    v_x1ps[i] = s2bv.inc(i1)
                    vector.wait_ge(s2bv.h, s2bv.v)
                    vector.wait_ge(s_x0b.h, 16 * (i + 1))
                    i2 = vector.tensor_tensor(r_x1b.f32, r_x1b.f32, r_x0b.f32, OP.add)
                    v_resid[i] = s2bv.inc(i2)
                    vector.wait_ge(s2bv.h, s2bv.v)
                    vector.reduce_sum(cc[:, 0:1], r_x1b.f32, axis=AXL.X)

                @block.scalar
                def _(scalar, i=i):
                    scalar.wait_ge(s2bv.h, v_resid[i])
                    s_x1w.inc(scalar.dma_start(out=x1s[128 * i:128 * (i + 1), :],
                                               in_=r_x1b.f32), 16)
                    scalar.activation(trashA, r_x1b.f32[:, 0:1024], AF.Square)
                    i2 = scalar.activation(r_attb.b16[:, 2048 * i:2048 * i + 1024],
                                           r_x1b.f32[:, 1024:2048], AF.Square)
                    s2bs.inc(i2)

                @block.vector
                def _(vector, i=i):
                    vector.wait_ge(s2bs.h, s2bs.v)
                    vector.reduce_sum(r_cols.f32[:, 6:7], trashA, axis=AXL.X)
                    i3 = vector.reduce_sum(r_cols.f32[:, 1:2],
                                           r_attb.b16[:, 2048 * i:2048 * i + 1024],
                                           axis=AXL.X)
                    vbar(vector, i3)
                    i4 = vector.tensor_tensor(r_cols.f32[:, 1:2], r_cols.f32[:, 1:2],
                                              r_cols.f32[:, 6:7], OP.add)
                    s2bv.inc(i4)

                ln_cols(s2bv, s2bs, D, rscale=0.5)

                @block.vector
                def _(vector, i=i):
                    cc = r_cols.f32
                    i1 = vector.tensor_scalar(r_attb.b16[:, 2048 * i:2048 * (i + 1)],
                                              r_x1b.f32, cc[:, 4:5], cc[:, 5:6],
                                              OP.mult, OP.add)
                    v_xn2[i] = s2bv.inc(i1)

                if i + 1 < NCH:
                    @block.sync
                    def _(sync, i=i):
                        sync.wait_ge(s2bv.h, v_resid[i])
                        s_x0b.inc(sync.dma_start(out=r_x0b.f32,
                                                 in_=x0[128 * (i + 1):128 * (i + 2), :]), 16)

                if gamma_on:
                    emit_gamma(6)

            if "x1" in dbg:
                s_d4 = sem("d4")

                @block.scalar
                def _(scalar):
                    scalar.wait_ge(s_x1w.h, s_x1w.v)
                    s_d4.inc(scalar.dma_start(out=dbg_ap("x1", [S, D]), in_=x1s), 16)

        if live("gamma"):
            emit_gamma(NGTOT)
            r_gtmp = A.alloc(256, "gtmp")
            s_gf = sem("gfin")

            @block.scalar
            def _(scalar):
                scalar.wait_ge(s_gvs.h, s_gvs.v)
                for w in range(3):
                    s_gf.inc(scalar.dma_start(out=gsc[w], in_=r_gsum.f32[:, w:w + 1]), 16)

            for w in range(3):
                @block.scalar
                def _(scalar, w=w):
                    scalar.wait_ge(s_gf.h, s_gf.v)
                    s_gf.inc(scalar.dma_start(out=r_gtmp.f32[0:1, 0:128],
                                              in_=gsc[w].rearrange("(a b) -> a b", a=1)), 16)

                @block.vector
                def _(vector, w=w):
                    vector.wait_ge(s_gf.h, s_gf.v)
                    cc = r_cols.f32
                    i1 = vector.reduce_sum(cc[0:1, 7:8], r_gtmp.f32[0:1, 0:128], axis=AXL.X)
                    vbar(vector, i1)
                    i2 = vector.tensor_scalar(cc[0:1, 7:8], cc[0:1, 7:8], 0.5 / NW,
                                              0.5 * EPS_Q, OP.mult, OP.add)
                    vbar(vector, i2)
                    i3 = vector.tensor_scalar(r_gtmp.f32[0:1, 0:128], r_gtmp.f32[0:1, 0:128],
                                              0.0, cc[0:1, 7:8], OP.mult, OP.add)
                    vbar(vector, i3)
                    i4 = vector.tensor_scalar(r_gtmp.f32[0:1, 128:256],
                                              r_gtmp.f32[0:1, 0:128], -1.0, None, OP.mult)
                    s_gf.inc(i4)

                @block.scalar
                def _(scalar, w=w):
                    scalar.wait_ge(s_gf.h, s_gf.v)
                    s_gf.inc(scalar.dma_start(out=gsc[4 + 2 * w],
                                              in_=r_gtmp.f32[0:1, 0:128]), 16)
                    s_gf.inc(scalar.dma_start(out=gsc[5 + 2 * w],
                                              in_=r_gtmp.f32[0:1, 128:256]), 16)
                    scalar.wait_ge(s_gf.h, s_gf.v)
                    s_gf.inc(scalar.dma_start(
                        out=r_tq.f32[:, 2 * w:2 * w + 1],
                        in_=gsc[4 + 2 * w].rearrange("(a b) -> a b", b=1)), 16)
                    s_gf.inc(scalar.dma_start(
                        out=r_tq.f32[:, 2 * w + 1:2 * w + 2],
                        in_=gsc[5 + 2 * w].rearrange("(a b) -> a b", b=1)), 16)
            gamma_ready = s_gf.v
            r_gtmp.free()
            r_gs[0].free()
            r_gs[1].free()

            if "tq" in dbg:
                s_d3 = sem("d3")

                @block.scalar
                def _(scalar):
                    scalar.wait_ge(s_gf.h, gamma_ready)
                    s_d3.inc(scalar.dma_start(out=dbg_ap("tq", [128, 8]), in_=r_tq.f32), 16)

        if live("xn2t"):
            r_att_t[0].free()
            r_att_t[1].free()
            r_wotb.free()
            r_wosl.free()
            r_x0b.free()
            r_x1b.free()
            r_xn2t = A.alloc_b16(16 * S, "xn2t")
            xn2t3 = r_xn2t.b16.rearrange("p (dc t) -> p dc t", t=S)
            s_xtt = sem("xtt")
            s_xte = sem("xte")

            for i in range(NCH):
                for g in range(4):
                    bk = 4 + (g % 2)

                    @block.tensor
                    def _(tensor, i=i, g=g, bk=bk):
                        if i == 0 and g == 0:
                            tensor.wait_ge(s2bv.h, v_xn2[NCH - 1])
                        gidx = i * 4 + g
                        if gidx >= 2:
                            tensor.wait_ge(s_xte.h, gidx - 1)
                        for d4 in range(4):
                            d = 4 * g + d4
                            mm = tensor.matmul(
                                bank(bk, 128 * d4, 128),
                                r_attb.b16[:, 2048 * i + 128 * d:2048 * i + 128 * (d + 1)],
                                identb, start=(d4 == 0), stop=(d4 == 3))
                        s_xtt.inc(mm)

                    @block.vector
                    def _(vector, i=i, g=g, bk=bk):
                        vector.wait_ge(s_xtt.h, s_xtt.v)
                        cp = vector.tensor_copy(
                            xn2t3[:, 4 * g:4 * g + 4, 128 * i:128 * (i + 1)],
                            bankspan(bk, 1).rearrange("p (a b) -> p a b", b=128))
                        s_xte.inc(cp)

            if "xn2t" in dbg:
                s_d5 = sem("d5")

                @block.scalar
                def _(scalar):
                    scalar.wait_ge(s_xte.h, s_xte.v)
                    s_d5.inc(scalar.dma_start(out=dbg_ap("xn2t", [128, 16 * S], BF16),
                                              in_=r_xn2t.b16), 16)

            r_attb.free()

        if live("p3"):
            r_q = [A.alloc_b16(2048, "q0"), A.alloc_b16(2048, "q1")]
            r_wstr = [A.alloc(2048, "wstrip0"), A.alloc(2048, "wstrip1")]
            r_sgb = A.alloc_b16(2048, "signb")
            r_w3q = [A.alloc_b16(1024, f"w3q{h}") for h in range(NHC)]
            r_w3s = [A.alloc(1024, "w3strip0"), A.alloc(1024, "w3strip1")]
            r_w3bb = A.alloc_b16(1024, "w3signb")
            r_hid = [A.alloc_b16(2048, "hid0"), A.alloc_b16(2048, "hid1")]
            r_sig = A.alloc(1024, "sig")

            s3ld = [sem("s3ld0"), sem("s3ld1")]
            s3qs = sem("s3qs")
            s3qv = sem("s3qv")
            s3t = sem("s3t")
            s3s = sem("s3s")
            s3v = sem("s3v")
            s3h = [sem("s3h0"), sem("s3h1")]
            s3w3d = [sem("s3w3d0"), sem("s3w3d1")]
            s3w3s = sem("s3w3s")
            s3w3v = sem("s3w3v")

            def strip_done_tval(s):
                hc, k = s // 2, s % 2
                return 4 * hc + 2 * (k + 1)

            xte_final = s_xte.v
            x1w_final = s_x1w.v

            for s in range(2 * NHC):
                hc, k = s // 2, s % 2
                wsrc = w1b if k == 0 else w2b

                @block.sync
                def _(sync, s=s, hc=hc, wsrc=wsrc):
                    if s <= 1:
                        sync.wait_ge(s_xte.h, xte_final)
                        sync.wait_ge(s_x1w.h, x1w_final)
                    else:
                        sync.wait_ge(s3qs.h, s - 1)
                    s3ld[s % 2].inc(sync.dma_start(
                        out=r_wstr[s % 2].f32.rearrange("p (a b) -> p a b", b=128),
                        in_=wsrc[hc].rearrange("(a p) b -> p a b", p=128)), 16)

                @block.scalar
                def _(scalar, s=s, k=k):
                    scalar.wait_ge(s3ld[s % 2].h, 16 * (s // 2 + 1))
                    if s == 0:
                        scalar.wait_ge(s_gf.h, gamma_ready)
                    if s >= 2:
                        scalar.wait_ge(s3t.h, strip_done_tval(s - 2))
                    if s >= 1:
                        scalar.wait_ge(s3qv.h, s)
                    w = k
                    scalar.activation(r_q[s % 2].b16, r_wstr[s % 2].f32, AF.Sign,
                                      bias=r_tq.f32[:, 2 * w + 1:2 * w + 2])
                    i2 = scalar.activation(r_sgb.b16, r_wstr[s % 2].f32, AF.Sign,
                                           bias=r_tq.f32[:, 2 * w:2 * w + 1])
                    s3qs.inc(i2)

                @block.vector
                def _(vector, s=s):
                    vector.wait_ge(s3qs.h, s + 1)
                    s3qv.inc(vector.tensor_tensor(r_q[s % 2].b16, r_q[s % 2].b16,
                                                  r_sgb.b16, OP.add))

                for th in range(2):
                    x1b0 = 0 if th == 0 else 4
                    x2b0 = 2 if th == 0 else 6
                    pb = x1b0 if k == 0 else x2b0

                    @block.tensor
                    def _(tensor, s=s, hc=hc, k=k, th=th, pb=pb):
                        tensor.wait_ge(s3qv.h, s + 1)
                        if s == 0 and th == 0:
                            tensor.wait_ge(s_xte.h, s_xte.v)
                        if k == 0 and hc > 0:
                            tensor.wait_ge(s3s.h, 2 * (hc - 1) + th + 1)
                        if k == 1 and hc > 0:
                            tensor.wait_ge(s3v.h, 2 * (hc - 1) + th + 1)
                        for d in range(16):
                            for b2 in range(2):
                                mm = tensor.matmul(
                                    bank(pb + b2),
                                    r_q[s % 2].b16[:, 128 * d:128 * (d + 1)],
                                    r_xn2t.b16[:, 2048 * d + 1024 * th + 512 * b2:
                                               2048 * d + 1024 * th + 512 * (b2 + 1)],
                                    start=(d == 0), stop=(d == 15))
                        s3t.inc(mm)

                if k == 1:
                    for th in range(2):
                        x1b0 = 0 if th == 0 else 4
                        x2b0 = 2 if th == 0 else 6
                        x1_done = 4 * hc + 1 + th
                        x2_done = 4 * hc + 3 + th

                        @block.scalar
                        def _(scalar, hc=hc, th=th, x1b0=x1b0, x1_done=x1_done):
                            scalar.wait_ge(s3t.h, x1_done)
                            if hc > 0 or th > 0:
                                scalar.wait_ge(s3v.h, 2 * hc + th)
                            s3s.inc(scalar.activation(r_sig.f32, bankspan(x1b0, 2),
                                                      AF.Tanh, scale=0.5))

                        @block.vector
                        def _(vector, hc=hc, th=th, x2b0=x2b0, x2_done=x2_done):
                            vector.wait_ge(s3t.h, x2_done)
                            vector.wait_ge(s3s.h, 2 * hc + th + 1)
                            if hc >= 2:
                                vector.wait_ge(s3h[hc % 2].h, 16 * (hc // 2))
                            i0 = vector.tensor_scalar(r_sig.f32, r_sig.f32, 0.5, 0.5,
                                                      OP.mult, OP.add)
                            vbar(vector, i0)
                            s3v.inc(vector.scalar_tensor_tensor(
                                r_hid[hc % 2].b16[:, 1024 * th:1024 * (th + 1)],
                                bankspan(x2b0, 2), 1.0, r_sig.f32, OP.mult, OP.mult))

                    @block.scalar
                    def _(scalar, hc=hc):
                        scalar.wait_ge(s3v.h, 2 * hc + 2)
                        s3h[hc % 2].inc(scalar.dma_start(out=hids[hc],
                                                         in_=r_hid[hc % 2].b16), 16)

                    @block.sync
                    def _(sync, hc=hc):
                        if hc <= 1:
                            sync.wait_ge(s_xte.h, xte_final)
                            sync.wait_ge(s_x1w.h, x1w_final)
                        else:
                            sync.wait_ge(s3w3v.h, hc - 1)
                        s3w3d[hc % 2].inc(sync.dma_start(out=r_w3s[hc % 2].f32,
                                                         in_=w3b[0, hc]), 16)

                    @block.vector
                    def _(vector, hc=hc):
                        vector.wait_ge(s3w3d[hc % 2].h, 16 * (hc // 2 + 1))
                        if hc == 0:
                            vector.wait_ge(s_gf.h, gamma_ready)
                        if hc >= 1:
                            vector.wait_ge(s3w3v.h, hc)
                        ia = vector.tensor_scalar(r_w3q[hc].b16, r_w3s[hc % 2].f32,
                                                  r_tq.f32[:, 4:5], None, OP.is_gt)
                        ib = vector.tensor_scalar(r_w3bb.b16, r_w3s[hc % 2].f32,
                                                  r_tq.f32[:, 5:6], None, OP.is_lt)
                        vbar(vector, ib)
                        s3w3v.inc(vector.tensor_tensor(r_w3q[hc].b16, r_w3q[hc].b16,
                                                       r_w3bb.b16, OP.subtract))

            if "q1s" in dbg:
                s_d6 = sem("d6")

                @block.scalar
                def _(scalar):
                    scalar.wait_ge(s3qv.h, 1)
                    s_d6.inc(scalar.dma_start(out=dbg_ap("q1s", [128, 2048], BF16),
                                              in_=r_q[0].b16), 16)

            if "hid" in dbg:
                s_d7 = sem("d7")

                @block.scalar
                def _(scalar):
                    scalar.wait_ge(s3h[0].h, s3h[0].v)
                    scalar.wait_ge(s3h[1].h, s3h[1].v)
                    s_d7.inc(scalar.dma_start(out=dbg_ap("hid", [NHC, 128, S], BF16),
                                              in_=hids), 16)

            for r in (r_q[0], r_q[1], r_wstr[0], r_wstr[1], r_sgb, r_hid[0], r_hid[1],
                      r_sig, r_xn2t):
                r.free()
            p3_h_final = (s3h[0].v, s3h[1].v)
            p3_t_final = s3t.v

        if live("p4"):
            r_stg4 = [A.alloc(1024, "p4stg0"), A.alloc(1024, "p4stg1")]
            r_p4st = A.alloc(32, "p4stats")
            r_hsl = [A.alloc_b16(11 * 128, f"hsl{j}") for j in range(3)]
            r_w3q2 = [A.alloc_b16(1024, f"w3q2_{h}") for h in range(NHC)]

            s4ld = [sem(f"s4ld{j}") for j in range(3)]
            s4st = [sem("s4st0"), sem("s4st1")]
            s4x1 = sem("s4x1")
            s4t = sem("s4t")
            s4tq = sem("s4tq")
            s4v = sem("s4v")
            s4s = sem("s4s")
            s4w = [sem("s4w0"), sem("s4w1")]
            s4y = sem("s4y")
            s4w3d = [sem("s4w3d0"), sem("s4w3d1")]
            s4w3s = sem("s4w3s")
            s4w3v = sem("s4w3v")

            v4ps = {}

            def emit_slice_dmas(qgs, first_wait_p3=False):
                @block.sync
                def _(sync):
                    for qg in qgs:
                        rnd, rem = divmod(qg, 64)
                        t1, q = divmod(rem, 4)
                        if first_wait_p3 and qg == 0:
                            sync.wait_ge(s3h[0].h, p3_h_final[0])
                            sync.wait_ge(s3h[1].h, p3_h_final[1])
                        if qg >= 3:
                            sync.wait_ge(s4tq.h, qg - 2)
                        src = hids[11 * q:11 * (q + 1), :, 128 * t1:128 * (t1 + 1)]
                        s4ld[qg % 3].inc(sync.dma_start(
                            out=r_hsl[qg % 3].b16.rearrange("p (a b) -> p a b", b=128),
                            in_=src.rearrange("a p b -> p a b")), 16)

            emit_slice_dmas(range(0, 4), first_wait_p3=True)

            if "hsl0" in dbg:
                s_da = sem("da")

                @block.scalar
                def _(scalar):
                    scalar.wait_ge(s4ld[0].h, 16)
                    s_da.inc(scalar.dma_start(out=dbg_ap("hsl0", [128, 1408], BF16),
                                              in_=r_hsl[0].b16), 16)

            for rnd in range(p4_rounds):
                if rnd == 1:
                    r_w3s[0].free()
                    r_w3s[1].free()
                    r_w3bb.free()
                    for _r in r_w3q:
                        _r.free()
                    r_stg4[0].free()
                    r_stg4[1].free()
                    r_sst = [A.alloc(1024, "rst0"), A.alloc(1024, "rst1")]
                    r_x1rd = A.alloc(2048, "x1rd")
                    r_yst = A.alloc(2048, "ystage")

                w3q_cur = r_w3q if rnd == 0 else r_w3q2

                for t1 in range(NCH):
                    ti = rnd * NCH + t1
                    pb = 0 if t1 % 2 == 0 else 2

                    if rnd == 1:
                        @block.sync
                        def _(sync, t1=t1):
                            if t1 == 0:
                                sync.wait_ge(s4tq.h, 4 * NCH)
                                sync.wait_ge(s4w[0].h, s4w[0].v)
                                sync.wait_ge(s4w[1].h, s4w[1].v)
                            if t1 >= 2:
                                sync.wait_ge(s4v.h, v4ps[NCH + t1 - 2])
                            s4st[t1 % 2].inc(sync.dma_start(out=r_sst[t1 % 2].f32,
                                                            in_=stash[t1]), 16)
                            if t1 >= 1:
                                sync.wait_ge(s4v.h, v4ps[NCH + t1 - 1] + 1)
                            s4x1.inc(sync.dma_start(out=r_x1rd.f32,
                                                    in_=x1s[128 * t1:128 * (t1 + 1), :]), 16)

                    @block.tensor
                    def _(tensor, rnd=rnd, t1=t1, ti=ti, pb=pb, w3q_cur=w3q_cur):
                        if ti >= 2:
                            tensor.wait_ge(s4v.h, v4ps[ti - 2])
                        if rnd == 0 and t1 == 0:
                            tensor.wait_ge(s3w3v.h, NHC)
                        if rnd == 1 and t1 == 0:
                            tensor.wait_ge(s4w3v.h, NHC)
                        for q in range(4):
                            qg = 64 * rnd + 4 * t1 + q
                            tensor.wait_ge(s4ld[qg % 3].h, 16 * (qg // 3 + 1))
                            buf = r_hsl[qg % 3]
                            if not (p4_parts & 4):
                                s4tq.inc(tensor.nop())
                                continue
                            for hh in range(11):
                                hc = 11 * q + hh
                                for b in range(2):
                                    mm = tensor.matmul(
                                        bank(pb + b),
                                        buf.b16[:, 128 * hh:128 * (hh + 1)],
                                        w3q_cur[hc].b16[:, 512 * b:512 * (b + 1)],
                                        start=(hc == 0), stop=(hc == NHC - 1))
                            s4tq.inc(mm)

                    nxt = 64 * rnd + 4 * (t1 + 1)
                    if rnd == 0 or t1 + 1 < NCH:
                        emit_slice_dmas(range(nxt, min(nxt + 4, 128)))

                    if rnd == 0 and not (p4_parts & 1):
                        @block.vector
                        def _(vector, t1=t1, ti=ti, pb=pb):
                            vector.wait_ge(s4tq.h, 4 * (ti + 1))
                            v4ps[ti] = s4v.inc(vector.memset(r_p4st.f32[:, 2 * t1:2 * t1 + 1], 0.0))
                    if rnd == 0 and (p4_parts & 1):
                        @block.vector
                        def _(vector, t1=t1, ti=ti, pb=pb):
                            vector.wait_ge(s4tq.h, 4 * (ti + 1))
                            if t1 >= 2:
                                vector.wait_ge(s4w[t1 % 2].h, 16 * (t1 // 2))
                            vector.tensor_copy(r_stg4[t1 % 2].f32, bankspan(pb, 2))
                            i2 = vector.reduce_sum(r_p4st.f32[:, 2 * t1:2 * t1 + 1],
                                                   bankspan(pb, 2), axis=AXL.X)
                            s4v.inc(i2)

                        sv_vrd = s4v.v

                        @block.scalar
                        def _(scalar, t1=t1, ti=ti, pb=pb, sv_vrd=sv_vrd):
                            scalar.wait_ge(s4v.h, sv_vrd)
                            s4s.inc(scalar.activation(trashA, bankspan(pb, 2), AF.Square))

                        @block.vector
                        def _(vector, t1=t1, ti=ti, pb=pb):
                            vector.wait_ge(s4s.h, s4s.v)
                            i3 = vector.reduce_sum(r_p4st.f32[:, 2 * t1 + 1:2 * t1 + 2],
                                                   trashA, axis=AXL.X)
                            v4ps[ti] = s4v.inc(i3)

                        @block.scalar
                        def _(scalar, t1=t1, ti=ti):
                            scalar.wait_ge(s4v.h, v4ps[ti])
                            s4w[t1 % 2].inc(scalar.dma_start(out=stash[t1],
                                                             in_=r_stg4[t1 % 2].f32), 16)


                        if t1 == 0 and "w3q0" in dbg:
                            s_d9 = sem("d9")

                            @block.scalar
                            def _(scalar):
                                scalar.wait_ge(s3w3v.h, NHC)
                                s_d9.inc(scalar.dma_start(
                                    out=dbg_ap("w3q0", [128, 1024], BF16),
                                    in_=r_w3q[0].b16), 16)
                                s_d9.inc(scalar.dma_start(
                                    out=dbg_ap("w3q7", [128, 1024], BF16),
                                    in_=r_w3q[7].b16), 16)

                        if t1 == NCH - 1 and "sth" in dbg:
                            s_d8 = sem("d8")

                            @block.scalar
                            def _(scalar):
                                scalar.wait_ge(s4w[0].h, s4w[0].v)
                                scalar.wait_ge(s4w[1].h, s4w[1].v)
                                s_d8.inc(scalar.dma_start(
                                    out=dbg_ap("sth", [NCH, 128, 1024]), in_=stash), 16)

                        for hc3 in (range(3 * t1, min(3 * (t1 + 1), NHC)) if (p4_parts & 2) else []):
                            @block.sync
                            def _(sync, hc3=hc3):
                                if hc3 <= 1:
                                    sync.wait_ge(s3w3v.h, NHC)
                                else:
                                    sync.wait_ge(s4w3v.h, hc3 - 1)
                                s4w3d[hc3 % 2].inc(sync.dma_start(
                                    out=r_w3s[hc3 % 2].f32, in_=w3b[1, hc3]), 16)

                            @block.vector
                            def _(vector, hc3=hc3):
                                vector.wait_ge(s4w3d[hc3 % 2].h, 16 * (hc3 // 2 + 1))
                                if hc3 == 0:
                                    vector.wait_ge(s3w3v.h, NHC)
                                    vector.wait_ge(s3t.h, p3_t_final)
                                else:
                                    vector.wait_ge(s4w3v.h, hc3)
                                ia = vector.tensor_scalar(r_w3q2[hc3].b16,
                                                          r_w3s[hc3 % 2].f32,
                                                          r_tq.f32[:, 4:5], None,
                                                          OP.is_gt)
                                ib = vector.tensor_scalar(r_w3bb.b16,
                                                          r_w3s[hc3 % 2].f32,
                                                          r_tq.f32[:, 5:6], None,
                                                          OP.is_lt)
                                vbar(vector, ib)
                                s4w3v.inc(vector.tensor_tensor(r_w3q2[hc3].b16,
                                                               r_w3q2[hc3].b16,
                                                               r_w3bb.b16,
                                                               OP.subtract))
                    if rnd == 1:
                        @block.vector
                        def _(vector, t1=t1, ti=ti, pb=pb):
                            vector.wait_ge(s4tq.h, 4 * (ti + 1))
                            cc = r_cols.f32
                            i1 = vector.reduce_sum(cc[:, 0:1], bankspan(pb, 2), axis=AXL.X)
                            s4v.inc(i1)
                            vector.wait_ge(s4v.h, s4v.v)
                            vector.tensor_tensor(cc[:, 0:1], cc[:, 0:1],
                                                 r_p4st.f32[:, 2 * t1:2 * t1 + 1], OP.add)

                        sv_vrd1 = s4v.v

                        @block.scalar
                        def _(scalar, t1=t1, ti=ti, pb=pb, sv_vrd1=sv_vrd1):
                            scalar.wait_ge(s4v.h, sv_vrd1)
                            s4s.inc(scalar.activation(trashA, bankspan(pb, 2), AF.Square))

                        @block.vector
                        def _(vector, t1=t1, ti=ti, pb=pb):
                            cc = r_cols.f32
                            vector.wait_ge(s4s.h, s4s.v)
                            i3 = vector.reduce_sum(cc[:, 1:2], trashA, axis=AXL.X)
                            vbar(vector, i3)
                            i4 = vector.tensor_tensor(cc[:, 1:2], cc[:, 1:2],
                                                      r_p4st.f32[:, 2 * t1 + 1:2 * t1 + 2],
                                                      OP.add)
                            s4v.inc(i4)

                        ln_cols(s4v, s4s, D)

                        @block.vector
                        def _(vector, t1=t1, ti=ti, pb=pb):
                            cc = r_cols.f32
                            vector.wait_ge(s4st[t1 % 2].h, 16 * (t1 // 2 + 1))
                            vector.wait_ge(s4x1.h, 16 * (t1 + 1))
                            if t1 > 0:
                                vector.wait_ge(s4y.h, 16 * t1)
                            vector.tensor_scalar(r_yst.f32[:, 0:1024], r_sst[t1 % 2].f32,
                                                 cc[:, 4:5], cc[:, 5:6], OP.mult, OP.add)
                            i2 = vector.tensor_scalar(r_yst.f32[:, 1024:2048],
                                                      bankspan(pb, 2),
                                                      cc[:, 4:5], cc[:, 5:6],
                                                      OP.mult, OP.add)
                            v4ps[ti] = s4v.inc(i2)
                            vector.wait_ge(s4v.h, s4v.v)
                            i3 = vector.tensor_tensor(r_yst.f32, r_yst.f32, r_x1rd.f32,
                                                      OP.add)
                            s4v.inc(i3)

                        @block.scalar
                        def _(scalar, t1=t1):
                            scalar.wait_ge(s4v.h, s4v.v)
                            s4y.inc(scalar.dma_start(out=y[128 * t1:128 * (t1 + 1), :],
                                                     in_=r_yst.f32), 16)

        @block.sync
        def _(sync):
            for _s in _sems.values():
                if _s.v > 0:
                    sync.wait_ge(_s.h, _s.v)

    return nc


def _host_pack(Wq, Wk, Wv, Wo, w1, w2, w3):
    wqkvT = np.concatenate([np.asarray(Wq, np.float32), np.asarray(Wk, np.float32),
                            np.asarray(Wv, np.float32)], 0)
    wqkv_b = np.ascontiguousarray(
        wqkvT.T.reshape(16, 128, NO, 128).transpose(2, 0, 1, 3))
    wo_t = np.ascontiguousarray(np.asarray(Wo, np.float32).T)
    w1t = np.asarray(w1, np.float32).T
    w2t = np.asarray(w2, np.float32).T
    w1bb = np.ascontiguousarray(w1t.reshape(D, NHC, 128).transpose(1, 0, 2))
    w2bb = np.ascontiguousarray(w2t.reshape(D, NHC, 128).transpose(1, 0, 2))
    w3t = np.asarray(w3, np.float32).T
    w3bb = np.ascontiguousarray(w3t.reshape(NHC, 128, 2, 1024).transpose(2, 0, 1, 3))
    return wqkv_b, wo_t, w1bb, w2bb, w3bb


def _tables():
    inv = 1.0 / (10000.0 ** (np.arange(0, HD, 2, dtype=np.float64) / HD))
    t = np.arange(S, dtype=np.float64)
    fr = np.outer(t, inv)
    emb = np.concatenate([fr, fr], -1)
    cos = np.cos(emb).T.astype(BF)
    sin = np.sin(emb).T.astype(np.float32)
    sin[:64] *= -1.0
    sinx_t = sin.astype(BF)
    maskut = np.triu(np.ones((C, C), np.float32))
    mask4m = np.tile(maskut, (1, 4)).astype(BF)
    identm = np.eye(128, dtype=np.float32).astype(BF)
    return (np.ascontiguousarray(cos), np.ascontiguousarray(sinx_t),
            np.ascontiguousarray(mask4m), np.ascontiguousarray(identm))


def kernel(L0_attn_x, L0_attn_Wq, L0_attn_Wk, L0_attn_Wv, L0_attn_Wo,
           L0_ffn_w1, L0_ffn_w2, L0_ffn_w3):
    x = np.ascontiguousarray(np.asarray(L0_attn_x, np.float32))
    wqkv_b, wo_t, w1bb, w2bb, w3bb = _host_pack(L0_attn_Wq, L0_attn_Wk, L0_attn_Wv,
                                                L0_attn_Wo, L0_ffn_w1, L0_ffn_w2,
                                                L0_ffn_w3)
    cos, sinx_t, mask4m, identm = _tables()
    nc = build_nc()
    shared = dict(wqkv=wqkv_b, wo=wo_t, w1b=w1bb, w2b=w2bb, w3b=w3bb,
                  cosf=cos, sinx=sinx_t, mask4=mask4m, ident=identm)
    in_maps = [dict(x0=np.ascontiguousarray(x[c]), **shared) for c in range(8)]
    res = run_bass_kernel_spmd(nc, in_maps, core_ids=list(range(8)))
    return np.stack([np.asarray(r["y"], np.float32) for r in res.results], 0)
